# version 02
# baseline (speedup 1.0000x reference)
import numpy as np
import ml_dtypes
import concourse.bass as bass
import concourse.mybir as mybir
from concourse.bass_utils import run_bass_kernel_spmd

F32 = mybir.dt.float32
BF16 = mybir.dt.bfloat16
AF = mybir.ActivationFunctionType
ALU = mybir.AluOpType
AX = mybir.AxisListType

T = 1024
TE = 1027
D = 2048
KT = 16
NH = 32
NC_ = 8
FT = 11
NE = 10
OT_A = 64
NCORES = 8

ENGINES = ("tensor", "vector", "scalar", "gpsimd", "sync")


class Rec:

    def __init__(self):
        self.ops = {e: [] for e in ENGINES}
        self.cnt = {}

    def op(self, eng, fn, waits=(), sig=None, inc=1):
        flat = []
        for w in waits:
            if w is None:
                continue
            if isinstance(w, list):
                flat.extend(x for x in w if x is not None)
            else:
                flat.append(w)
        waits = flat
        evt = None
        if sig is not None:
            self.cnt[sig] = self.cnt.get(sig, 0) + inc
            evt = (sig, self.cnt[sig])
        self.ops[eng].append((waits, fn, sig, inc))
        return evt

    def emit(self, nc, sems):
        hw = {}

        def runner(ename):
            def cb(eng):
                for waits, fn, sig, inc in self.ops[ename]:
                    for (s, v) in waits:
                        if hw.get((ename, s), -1) < v:
                            eng.wait_ge(sems[s], v)
                            hw[(ename, s)] = v
                    inst = fn(eng)
                    if sig is not None:
                        assert inst is not None, f"sig {sig} op returned None"
                        inst.then_inc(sems[sig], inc)
            return cb

        with nc.Block() as block:
            block.tensor(runner("tensor"))
            block.vector(runner("vector"))
            block.scalar(runner("scalar"))
            block.gpsimd(runner("gpsimd"))
            block.sync(runner("sync"))


class PsumRing:
    def __init__(self, views):
        self.views = views
        self.r = 0
        self.rel_ev = [None] * len(views)

    def acq(self):
        i = self.r % len(self.views)
        self.r += 1
        return self.views[i], self.rel_ev[i], i

    def rel(self, i, event):
        self.rel_ev[i] = event


def build_program(dbg_taps=(), phases=99, p2_lvl=99):
    nc = bass.Bass("TRN2", target_bir_lowering=False)
    R = Rec()

    def din(name, shape, dtype=F32):
        return nc.dram_tensor(name, list(shape), dtype, kind="ExternalInput")

    d_u = din("uext", [TE, D])
    d_wins = din("wins", [65, 128, KT, 128])
    d_wouts = din("wouts", [16, 128, KT, 128])
    d_w13 = din("w13", [NE, FT, 128, 32, 128])
    d_w2 = din("w2q", [NE, 4, 128, FT, 512])
    d_diag = din("diag", [48, 128, 4, 128])
    d_idf = din("idf", [128, 128])
    d_idb = din("idb", [128, 128], BF16)
    d_cmask = din("cmask", [128, 128])
    d_cols = din("cols", [128, 64])
    d_convb = din("convb", [128, 48])
    d_gatew = din("gatew", [128, 128])
    d_selp = din("selp", [32, 2048])
    d_chain = din("chain", [8, 8])
    d_mk = din("mkcol", [8, 1])

    d_out = nc.dram_tensor("out", [T, D], F32, kind="ExternalOutput")
    d_g = nc.dram_tensor("gscratch", [D, T], F32)
    d_sin = nc.dram_tensor("cc_s_in", [129, 1024], F32)
    d_sout = nc.dram_tensor("cc_s_out", [8 * 129, 1024], F32, addr_space="Shared")
    d_dbg = {}
    for name in dbg_taps:
        pass

    TOP = 229344

    def alloc(name, shape, dtype, at):
        size = int(np.prod(shape[1:])) * mybir.dt.size(dtype)
        assert at + size <= TOP, f"SBUF overflow {name}: {at}+{size}"
        return nc.alloc_sbuf_tensor_at(name, list(shape), dtype, offset=at)

    p = 20480
    def seq(name, shape, dtype):
        nonlocal p
        h = alloc(name, shape, dtype, p)
        p += ((int(np.prod(shape[1:])) * mybir.dt.size(dtype) + 31) // 32) * 32
        return h

    IDF = seq("s_idf", [128, 128], F32)
    IDB = seq("s_idb", [128, 128], BF16)
    CMASK = seq("s_cmask", [128, 128], F32)
    COLS = seq("s_cols", [128, 64], F32)
    CONVB = seq("s_convb", [128, 48], F32)
    GATEW = seq("s_gatew", [128, 128], BF16)
    SELP = seq("s_selp", [32, 2048], F32)
    CHAIN = seq("s_chain", [8, 8], F32)
    MKC = seq("s_mkcol", [8, 1], F32)
    XN = seq("s_xn", [128, KT * TE], BF16)
    GBASE = p
    assert GBASE <= 66000, GBASE

    Y = alloc("s_y", [128, 16 * T], F32, GBASE + 0)
    HE = alloc("s_he", [128, FT * T], BF16, GBASE + 65536)
    W2R = [alloc(f"s_w2r{i}", [128, FT * 512], BF16, GBASE + 88064 + i * 11264)
           for i in range(2)]
    W13R = [alloc(f"s_w13r{i}", [128, 4096], BF16, GBASE + 110592 + i * 8192)
            for i in range(2)]
    CWBC = alloc("s_cwbc", [128, T], BF16, GBASE + 126976)
    SP1 = alloc("s_sp1", [128, T], BF16, GBASE + 129024)
    P3C = alloc("s_p3c", [128, T], BF16, GBASE + 131072)
    SCF = alloc("s_scf", [8, T], F32, GBASE + 133120)
    SCT = alloc("s_sct", [128, 64], F32, GBASE + 137216)
    CWTM = alloc("s_cwtm", [128, 64], F32, GBASE + 137472)
    CWT = alloc("s_cwt", [8, T], BF16, GBASE + 137728)
    GCOL = alloc("s_gcol", [128, 16], F32, GBASE + 139776)
    UB = alloc("s_ub", [128, D], F32, GBASE + 139840)
    XTMP = alloc("s_xtmp", [128, D], BF16, GBASE + 148032)
    LNC = alloc("s_lnc", [128, 16], F32, GBASE + 152128)
    GTMP = [alloc(f"s_gtmp{i}", [128, T], F32, GBASE + 152192 + i * 4096)
            for i in range(2)]
    assert GBASE + 160384 <= TOP

    XF = alloc("s_xf", [128, 16 * T], BF16, GBASE + 0)
    BFm = alloc("s_bfm", [128, 16 * T], BF16, GBASE + 32768)
    CFm = alloc("s_cfm", [128, 16 * T], BF16, GBASE + 65536)
    ZS = alloc("s_zs", [128, 16 * T], BF16, GBASE + 98304)
    AAR = alloc("s_a", [32, T], F32, GBASE + 131072)
    HST = alloc("s_h", [128, T], F32, GBASE + 135168)
    HBF = alloc("s_hbf", [128, T], BF16, GBASE + 139264)
    MIXZ = GBASE + 141312
    WINR = [alloc(f"s_winr{i}", [128, 2048], BF16, MIXZ + i * 4096) for i in range(3)]
    DIAGR = [alloc(f"s_diagr{i}", [128, 512], BF16, MIXZ + 12288 + i * 1024)
             for i in range(2)]
    XBCS = [alloc("s_xbcs0", [128, 1040], BF16, MIXZ + 14336)]
    ASCR = alloc("s_ascr", [32, T], F32, MIXZ + 16416)
    XTM = alloc("s_xtm", [128, 2048], BF16, MIXZ + 0)
    BTM = alloc("s_btm", [128, 2048], BF16, MIXZ + 4096)
    CTLD = alloc("s_ctld", [128, 2048], BF16, MIXZ + 8192)
    SJB = alloc("s_sjb", [128, 1024], F32, MIXZ + 8192)
    MTR = [alloc(f"s_mtr{i}", [128, 512], BF16, MIXZ + 12288 + i * 1024)
           for i in range(2)]
    AGS = alloc("s_ags", [8, 32], F32, MIXZ + 12288)
    WJS = alloc("s_wjs", [8, 32], F32, MIXZ + 12416)
    WTS = alloc("s_wts", [32, 8], F32, MIXZ + 12544)
    WCOLS = alloc("s_wcols", [128, 128], F32, MIXZ + 12576)
    DS2 = alloc("s_ds2", [128, 512], F32, MIXZ + 14336)
    ES2 = alloc("s_es2", [128, 512], F32, MIXZ + 16384)
    QTC = alloc("s_qtc", [128, 32], F32, MIXZ + 18432)
    DBT = alloc("s_dbt", [128, 32], F32, MIXZ + 18560)
    ALC = alloc("s_alc", [128, 16], F32, MIXZ + 18688)
    QCH = alloc("s_qch", [32, 128], F32, MIXZ + 18752)
    DBS = alloc("s_dbs", [32, 128], F32, MIXZ + 19264)
    ECS = alloc("s_ecs", [32, 128], F32, MIXZ + 19776)
    ACOL = alloc("s_acol", [32, 1], F32, MIXZ + 20288)
    assert MIXZ + 20512 <= TOP
    UALL = alloc("s_uall", [128, 8 * D], F32, GBASE + 32768)
    SGT = alloc("s_sgt", [128, T], F32, GBASE + 98304)
    GBUF = [alloc(f"s_gbuf{i}", [128, T], F32, GBASE + 102400 + i * 4096)
            for i in range(2)]
    WOR = [alloc(f"s_wor{i}", [128, 2048], BF16, GBASE + 110592 + i * 4096)
           for i in range(3)]
    OPI = [alloc(f"s_opi{i}", [128, 128], F32, GBASE + 122880 + i * 512)
           for i in range(4)]

    PPA = nc.alloc_psum_tensor("ppa", [128, 1024], F32)
    PPB = nc.alloc_psum_tensor("ppb", [128, 1024], F32)
    PPC = nc.alloc_psum_tensor("ppc", [128, 1024], F32)
    PT = nc.alloc_psum_tensor("ppt", [128, 2048], BF16)
    ring = PsumRing([PPA[:, 0:512], PPA[:, 512:1024], PPB[:, 0:512],
                     PPB[:, 512:1024], PPC[:, 0:512], PPC[:, 512:1024]])

    taps = []

    def tap(name, ap, waits):
        if name in dbg_taps:
            dt = nc.dram_tensor("dbg_" + name, list(ap.shape), ap.dtype,
                                kind="ExternalOutput")
            taps.append((name, dt, ap, [w for w in waits if w is not None]))

    def snap(name, ap, waits):
        if name not in dbg_taps:
            return None
        dt = nc.dram_tensor("dbg_" + name, list(ap.shape), ap.dtype,
                            kind="ExternalOutput")
        return R.op("sync", lambda e, dt=dt, ap=ap: e.dma_start(
            dt[tuple(slice(0, s) for s in ap.shape)], ap),
            waits=waits, sig=usem(), inc=16)

    def ld(dst, src):
        return lambda e, d=dst, s=src: e.dma_start(d, s)

    _wc = [0]
    _uc = [0]

    def wsem():
        _wc[0] += 1
        return f"s_w{_wc[0] % 3}"

    def usem():
        _uc[0] += 1
        return f"s_u{_uc[0] % 3}"

    def _finish(n_out_=0):
        tap_done = []
        for name, dtt, ap, waits in taps:
            tap_done.append(R.op("sync", lambda e, dtt=dtt, ap=ap: e.dma_start(
                dtt[tuple(slice(0, s) for s in ap.shape)], ap), waits=waits,
                sig=usem(), inc=16))
        fw = [(s, R.cnt.get(s, 0)) for s in
              ("s_u0", "s_u1", "s_u2", "s_w0", "s_w1", "s_w2")]
        if n_out_:
            fw.append(("s_out", 16 * n_out_))
        R.op("sync", lambda e: None, waits=fw)
        return nc, R

    ev_const = None
    for dst, src in [(IDF[:, :], d_idf[:, :]), (IDB[:, :], d_idb[:, :]),
                     (CMASK[:, :], d_cmask[:, :]), (COLS[:, :], d_cols[:, :]),
                     (CONVB[:, :], d_convb[:, :]), (SELP[:, :], d_selp[:, :]),
                     (CHAIN[:, :], d_chain[:, :]), (MKC[:, :], d_mk[:, :])]:
        ev_const = R.op("sync", ld(dst, src), sig=usem(), inc=16)
    ev_gatew = R.op("gpsimd", ld(GATEW[:, :], d_gatew[:, :]), sig=wsem(), inc=16)

    tiles = [(i * 128, min(TE, (i + 1) * 128)) for i in range(9)]
    ev_ubuf_free = None
    ev_pt_free = None
    ev_xtmp_free = None
    ev_rstd_prev = None
    ev_xn = None
    for ti, (r0, r1) in enumerate(tiles):
        P = r1 - r0
        ub = UB
        ev_u = R.op("sync", ld(ub[0:P, :], d_u[r0:r1, :]),
                    waits=[ev_ubuf_free], sig=usem(), inc=16)

        ev_red = R.op("vector", lambda e, ub=ub, P=P: e.tensor_reduce(
            LNC[0:P, 0:1], ub[0:P, :], AX.X, ALU.add, negate=True),
            waits=[ev_u, ev_rstd_prev], sig="s_ev")
        ev_mg = R.op("gpsimd", lambda e, P=P: e.tensor_scalar(
            LNC[0:P, 1:2], LNC[0:P, 0:1], 1.0 / D, None, ALU.mult),
            waits=[ev_red], sig="s_gp")
        ev_ms = R.op("vector", lambda e, ub=ub, P=P: e.tensor_scalar(
            ub[0:P, :], ub[0:P, :], LNC[0:P, 1:2], None, ALU.add),
            waits=[ev_mg], sig="s_ev")
        ev_sq = R.op("scalar", lambda e, ub=ub, P=P: e.activation(
            XTMP[0:P, :], ub[0:P, :], AF.Square),
            waits=[ev_ms, ev_xtmp_free], sig="s_sc")
        ev_s2 = R.op("vector", lambda e, P=P: e.tensor_reduce(
            LNC[0:P, 2:3], XTMP[0:P, :], AX.X, ALU.add), waits=[ev_sq], sig="s_ev")

        def rstd(e, P=P):
            e.activation(LNC[0:P, 3:4], LNC[0:P, 2:3], AF.Ln, scale=1.0 / D,
                         bias=COLS[0:P, 0:1])
            return e.activation(LNC[0:P, 4:5], LNC[0:P, 3:4], AF.Exp, scale=-0.5)
        ev_rstd = R.op("scalar", rstd, waits=[ev_s2, ev_const], sig="s_sc")
        ev_rstd_prev = ev_rstd
        ev_xt = R.op("vector", lambda e, ub=ub, P=P: e.tensor_scalar(
            XTMP[0:P, :], ub[0:P, :], LNC[0:P, 4:5], None,
            ALU.mult), waits=[ev_rstd], sig="s_ev")
        ev_ubuf_free = ev_xt

        def transposes(e, P=P):
            inst = None
            for kt in range(KT):
                inst = e.transpose(PT[:, kt * 128:kt * 128 + P],
                                   XTMP[0:P, kt * 128:(kt + 1) * 128], IDB[0:P, 0:P])
            return inst
        ev_tp = R.op("tensor", transposes, waits=[ev_xt, ev_pt_free, ev_const],
                     sig="s_mm")
        ev_xtmp_free = ev_tp

        def evac(e, r0=r0, P=P):
            src = PT.ap().rearrange("p (kt c) -> p kt c", kt=KT)[:, :, 0:P]
            dst = XN.ap().rearrange("p (kt c) -> p kt c", kt=KT)[:, :, r0:r0 + P]
            return e.tensor_copy(dst, src)
        ev_xn = R.op("vector", evac, waits=[ev_tp], sig="s_ev")
        ev_pt_free = ev_xn
    tap("xn", XN[:, :], [ev_xn])
    if phases <= 1:
        return _finish()

    b0, w0, t0 = ring.acq()
    b1, w1, t1 = ring.acq()

    def gate_mm(e, b0=b0, b1=b1):
        inst = None
        for nch, bb in ((0, b0), (1, b1)):
            for kt in range(KT):
                inst = e.matmul(bb[0:8, :], GATEW[:, kt * 8:(kt + 1) * 8],
                                XN[:, kt * TE + 3 + nch * 512: kt * TE + 3 + nch * 512 + 512],
                                start=(kt == 0), stop=(kt == KT - 1))
        return inst
    ev_gmm = R.op("tensor", gate_mm, waits=[ev_xn, ev_gatew, w0, w1], sig="s_mm")

    def gevac(e, b0=b0, b1=b1):
        e.tensor_copy(SCF[:, 0:512], b0[0:8, :])
        return e.tensor_copy(SCF[:, 512:1024], b1[0:8, :])
    ev_gse = R.op("vector", gevac, waits=[ev_gmm], sig="s_ev")
    ring.rel(t0, ev_gse); ring.rel(t1, ev_gse)

    b0, w0, t0 = ring.acq()
    def gtp(e, b0=b0):
        inst = None
        for tt in range(8):
            inst = e.transpose(b0[:, tt * 8:(tt + 1) * 8],
                               SCF[0:8, tt * 128:(tt + 1) * 128], IDF[0:8, 0:8])
        return inst
    ev_gtp = R.op("tensor", gtp, waits=[ev_gse, w0], sig="s_mm")
    ev_sct = R.op("vector", lambda e, b0=b0: e.tensor_copy(SCT[:, :], b0[:, 0:64]),
                  waits=[ev_gtp], sig="s_ev")
    ring.rel(t0, ev_sct)

    def smax_a(e):
        inst = None
        for tt in range(8):
            inst = e.tensor_reduce(GCOL[:, tt:tt + 1], SCT[:, tt * 8:(tt + 1) * 8],
                                   AX.X, ALU.max, negate=True)
        return inst
    ev_nm = R.op("vector", smax_a, waits=[ev_sct], sig="s_ev")

    def smax_b(e):
        inst = None
        for tt in range(8):
            inst = e.activation(SCT[:, tt * 8:(tt + 1) * 8], SCT[:, tt * 8:(tt + 1) * 8],
                                AF.Exp, bias=GCOL[:, tt:tt + 1],
                                accum_out=GCOL[:, 8 + tt:9 + tt])
        return inst
    ev_ex = R.op("scalar", smax_b, waits=[ev_nm], sig="s_sc")

    S8 = lambda tt: SCT[:, tt * 8:(tt + 1) * 8]
    M8 = lambda tt: CWTM[:, tt * 8:(tt + 1) * 8]

    def stepf(fn):
        def g(e):
            inst = None
            for tt in range(8):
                inst = fn(e, tt)
            return inst
        return g
    ev = R.op("vector", stepf(lambda e, tt: e.reciprocal(
        GCOL[:, tt:tt + 1], GCOL[:, 8 + tt:9 + tt])), waits=[ev_ex], sig="s_ev")
    ev = R.op("gpsimd", stepf(lambda e, tt: e.tensor_scalar(
        S8(tt), S8(tt), GCOL[:, tt:tt + 1], None, ALU.mult)), waits=[ev], sig="s_gp")
    ev = R.op("vector", stepf(lambda e, tt: e.tensor_reduce(
        GCOL[:, tt:tt + 1], S8(tt), AX.X, ALU.max)), waits=[ev], sig="s_ev")
    ev = R.op("gpsimd", stepf(lambda e, tt: e.tensor_scalar(
        M8(tt), S8(tt), GCOL[:, tt:tt + 1], None, ALU.is_ge)), waits=[ev], sig="s_gp")
    ev = R.op("vector", stepf(lambda e, tt: e.scalar_tensor_tensor(
        M8(tt), M8(tt), -2.0, S8(tt), ALU.mult, ALU.add)), waits=[ev], sig="s_ev")
    ev = R.op("vector", stepf(lambda e, tt: e.tensor_reduce(
        GCOL[:, tt:tt + 1], M8(tt), AX.X, ALU.max)), waits=[ev], sig="s_ev")
    ev = R.op("gpsimd", stepf(lambda e, tt: e.tensor_scalar(
        M8(tt), S8(tt), GCOL[:, tt:tt + 1], None, ALU.is_ge)), waits=[ev], sig="s_gp")
    ev_cw = R.op("vector", stepf(lambda e, tt: e.tensor_tensor(
        S8(tt), S8(tt), M8(tt), ALU.mult)), waits=[ev], sig="s_ev")
    tap("cw", SCT[:, :], [ev_cw])

    b0, w0, t0 = ring.acq()
    b1, w1, t1 = ring.acq()
    def cwtp(e, b0=b0, b1=b1):
        inst = None
        for tt in range(8):
            bb = b0 if tt < 4 else b1
            inst = e.transpose(bb[0:8, (tt % 4) * 128:(tt % 4 + 1) * 128],
                               SCT[:, tt * 8:(tt + 1) * 8], IDF[:, :])
        return inst
    ev_cwtp = R.op("tensor", cwtp, waits=[ev_cw, w0, w1], sig="s_mm")

    def cwte(e, b0=b0, b1=b1):
        e.tensor_copy(CWT[:, 0:512], b0[0:8, :])
        return e.tensor_copy(CWT[:, 512:1024], b1[0:8, :])
    ev_cwt = R.op("vector", cwte, waits=[ev_cwtp], sig="s_ev")
    ring.rel(t0, ev_cwt); ring.rel(t1, ev_cwt)

    ev_w13 = [None] * 2
    ev_w13_free = [None] * 2
    ev_w2 = [None, None]
    ev_w2_free = [None, None]
    ev_y = None
    ev_pe_prev = None
    ev_he_prev = None
    slot_i = 0
    for e_i in range(NE):
        routed = e_i >= 2
        ev_cwbc = None
        if routed:
            re = e_i - 2
            bc0, wc0, tc0 = ring.acq()
            bc1, wc1, tc1 = ring.acq()

            def cwbc_mm(e, re=re, bc0=bc0, bc1=bc1):
                sel = IDB[0:8, re:re + 1].broadcast_to([8, 128])
                e.matmul(bc0[:, :], sel, CWT[0:8, 0:512], start=True, stop=True)
                return e.matmul(bc1[:, :], sel, CWT[0:8, 512:1024], start=True, stop=True)
            ev_bc = R.op("tensor", cwbc_mm, waits=[ev_cwt, wc0, wc1], sig="s_mm")

            def cwbc_ev(e, bc0=bc0, bc1=bc1):
                e.tensor_copy(CWBC[:, 0:512], bc0[:, :])
                return e.tensor_copy(CWBC[:, 512:1024], bc1[:, :])
            ev_cwbc = R.op("vector", cwbc_ev, waits=[ev_bc], sig="s_ev")
            ring.rel(tc0, ev_cwbc); ring.rel(tc1, ev_cwbc)

        ev_he_all = []
        for ft in range(FT):
            sl = slot_i % 2
            slot_i += 1
            ev_w13[sl] = R.op("gpsimd", ld(
                W13R[sl][:, :], d_w13[e_i, ft].rearrange("p a b -> p (a b)")),
                waits=[ev_w13_free[sl]], sig=wsem(), inc=16)
            bq = [ring.acq() for _ in range(4)]

            def pmm(e, sl=sl, bq=bq):
                inst = None
                for half in range(2):
                    off = 3 + half * 512
                    for kt in range(KT):
                        e.matmul(bq[half][0][:, :], W13R[sl][:, kt * 128:(kt + 1) * 128],
                                 XN[:, kt * TE + off: kt * TE + off + 512],
                                 start=(kt == 0), stop=(kt == KT - 1))
                    for kt in range(KT):
                        inst = e.matmul(bq[2 + half][0][:, :],
                                        W13R[sl][:, 2048 + kt * 128: 2048 + (kt + 1) * 128],
                                        XN[:, kt * TE + off: kt * TE + off + 512],
                                        start=(kt == 0), stop=(kt == KT - 1))
                return inst
            ev_pmm = R.op("tensor", pmm,
                          waits=[ev_w13[sl], ev_xn] + [b[1] for b in bq], sig="s_mm")
            ev_w13_free[sl] = ev_pmm

            def pevac(e, bq=bq):
                e.activation(SP1[:, 0:512], bq[0][0][:, :], AF.Silu)
                e.activation(SP1[:, 512:1024], bq[1][0][:, :], AF.Silu)
                e.activation(P3C[:, 0:512], bq[2][0][:, :], AF.Copy)
                return e.activation(P3C[:, 512:1024], bq[3][0][:, :], AF.Copy)
            ev_pe = R.op("scalar", pevac, waits=[ev_pmm, ev_he_prev], sig="s_sc")
            ev_pe_prev = ev_pe
            for b in bq:
                ring.rel(b[2], ev_pe)

            if routed:
                def hmul(e, ft=ft):
                    e.tensor_tensor(HE[:, ft * T:(ft + 1) * T], SP1[:, :], P3C[:, :],
                                    ALU.mult)
                    return e.tensor_tensor(HE[:, ft * T:(ft + 1) * T],
                                           HE[:, ft * T:(ft + 1) * T], CWBC[:, :],
                                           ALU.mult)
                ev_he = R.op("vector", hmul, waits=[ev_pe, ev_cwbc], sig="s_ev")
            else:
                def hmul(e, ft=ft):
                    return e.tensor_tensor(HE[:, ft * T:(ft + 1) * T], SP1[:, :],
                                           P3C[:, :], ALU.mult)
                ev_he = R.op("vector", hmul, waits=[ev_pe], sig="s_ev")
            ev_he_prev = ev_he
            ev_he_all.append(ev_he)

        for oq in range(4):
            sl2 = oq % 2
            ev_w2[sl2] = R.op("gpsimd", ld(
                W2R[sl2][:, :], d_w2[e_i, oq].rearrange("p a b -> p (a b)")),
                waits=[ev_w2_free[sl2]], sig=wsem(), inc=16)
            for tq in range(4):
                bk = [ring.acq(), ring.acq()]

                def w2mm(e, sl2=sl2, tq=tq, bk=bk):
                    inst = None
                    for osub in range(4):
                        tgt = bk[osub // 2][0][:, (osub % 2) * 256:(osub % 2) * 256 + 256]
                        for ft in range(FT):
                            inst = e.matmul(
                                tgt,
                                W2R[sl2][:, ft * 512 + osub * 128: ft * 512 + (osub + 1) * 128],
                                HE[:, ft * T + tq * 256: ft * T + tq * 256 + 256],
                                start=(ft == 0), stop=(ft == FT - 1))
                    return inst
                ev_wmm = R.op("tensor", w2mm,
                              waits=[ev_w2[sl2], ev_he_all[-1]] + [b[1] for b in bk],
                              sig="s_mm")
                if tq == 3:
                    ev_w2_free[sl2] = ev_wmm

                def yacc(e, oq=oq, tq=tq, bk=bk, first=(e_i == 0)):
                    inst = None
                    for half in range(2):
                        srcv = bk[half][0].rearrange("p (o t) -> p o t", o=2)
                        dst = Y.ap().rearrange("p (o t) -> p o t", o=16)[
                            :, 4 * oq + 2 * half: 4 * oq + 2 * half + 2,
                            tq * 256: tq * 256 + 256]
                        if first:
                            inst = e.tensor_copy(dst, srcv)
                        else:
                            inst = e.scalar_tensor_tensor(dst, srcv, 1.0, dst,
                                                          ALU.mult, ALU.add)
                    return inst
                ev_y = R.op("vector", yacc, waits=[ev_wmm], sig="s_ev")
                for b in bk:
                    ring.rel(b[2], ev_y)
    tap("yffn", Y[:, :], [ev_y])

    ev_g_store = []
    ev_gt_free = [None, None]
    for ot in range(16):
        gb = GTMP[ot % 2]
        ev_sg = R.op("scalar", lambda e, ot=ot, gb=gb: e.activation(
            gb[:, :], Y[:, ot * T:(ot + 1) * T], AF.Silu),
            waits=[ev_y, ev_gt_free[ot % 2]], sig="s_sc")
        ev_st = R.op("sync", lambda e, ot=ot, gb=gb: e.dma_start(
            d_g[ot * 128:(ot + 1) * 128, :], gb[:, :]), waits=[ev_sg], sig=usem(), inc=16)
        ev_gt_free[ot % 2] = ev_st
        ev_g_store.append(ev_st)
    if phases <= 2:
        return _finish()

    gate_mix = ev_g_store[-1]
    ev_win = [None] * 3
    ev_win_free = [gate_mix] * 3
    ev_diag = [None] * 2
    ev_diag_free = [gate_mix] * 2
    ev_xbcs_free = gate_mix
    ev_conv_done = {}
    ev_a_ready = None
    win_i = 0
    for oi, ot in enumerate(list(range(48)) + [OT_A]):
        sl = win_i % 3
        win_i += 1
        ev_win[sl] = R.op("gpsimd", ld(
            WINR[sl][:, :], d_wins[ot].rearrange("p a b -> p (a b)")),
            waits=[ev_win_free[sl]], sig=wsem(), inc=16)
        is_a = (ot == OT_A)
        if not is_a:
            dsl = oi % 2
            ev_diag[dsl] = R.op("gpsimd", ld(
                DIAGR[dsl][:, :], d_diag[ot].rearrange("p a b -> p (a b)")),
                waits=[ev_diag_free[dsl]], sig=wsem(), inc=16)
        bm = [ring.acq(), ring.acq(), ring.acq()]

        def projmm(e, sl=sl, bm=bm):
            inst = None
            for (bi, off, n) in ((0, 0, 512), (1, 512, 512), (2, 1024, 3)):
                for kt in range(KT):
                    inst = e.matmul(bm[bi][0][:, 0:n],
                                    WINR[sl][:, kt * 128:(kt + 1) * 128],
                                    XN[:, kt * TE + off: kt * TE + off + n],
                                    start=(kt == 0), stop=(kt == KT - 1))
            return inst
        ev_proj = R.op("tensor", projmm,
                       waits=[ev_win[sl], ev_xn] + [b[1] for b in bm], sig="s_mm")
        ev_win_free[sl] = ev_proj

        if is_a:
            def aev(e, bm=bm):
                e.tensor_copy(ASCR[0:32, 0:509], bm[0][0][0:32, 3:512])
                e.tensor_copy(ASCR[0:32, 509:1021], bm[1][0][0:32, :])
                return e.tensor_copy(ASCR[0:32, 1021:1024], bm[2][0][0:32, 0:3])
            ev_acp = R.op("vector", aev, waits=[ev_proj, gate_mix], sig="s_ev")
            for b in bm:
                ring.rel(b[2], ev_acp)

            def soft1(e):
                e.activation(AAR[0:32, :], ASCR[0:32, :], AF.Abs)
                e.activation(AAR[0:32, :], AAR[0:32, :], AF.Exp, scale=-1.0)
                return e.activation(AAR[0:32, :], AAR[0:32, :], AF.Ln,
                                    bias=COLS[0:32, 1:2])
            ev_sp = R.op("scalar", soft1, waits=[ev_acp, gate_mix], sig="s_sc")

            def soft2(e):
                e.tensor_scalar(ASCR[0:32, :], ASCR[0:32, :], 0.0, None, ALU.max)
                return e.scalar_tensor_tensor(AAR[0:32, :], AAR[0:32, :], -1.0,
                                              ASCR[0:32, :], ALU.mult, ALU.subtract)
            ev_a_ready = R.op("vector", soft2, waits=[ev_sp], sig="s_ev")
            continue

        xb = XBCS[0]
        def pevac2(e, bm=bm, xb=xb):
            e.tensor_copy(xb[:, 0:512], bm[0][0][:, :])
            e.tensor_copy(xb[:, 512:1024], bm[1][0][:, :])
            return e.tensor_copy(xb[:, 1024:1027], bm[2][0][:, 0:3])
        ev_xev = R.op("vector", pevac2, waits=[ev_proj, ev_xbcs_free],
                      sig="s_ev")
        for b in bm:
            ring.rel(b[2], ev_xev)

        bc = [ring.acq(), ring.acq()]
        dsl = oi % 2
        def convmm(e, xb=xb, dsl=dsl, bc=bc):
            inst = None
            for (bi, c0) in ((0, 0), (1, 512)):
                for k in range(4):
                    inst = e.matmul(bc[bi][0][:, :], DIAGR[dsl][:, k * 128:(k + 1) * 128],
                                    xb[:, k + c0: k + c0 + 512],
                                    start=(k == 0), stop=(k == 3))
            return inst
        ev_conv = R.op("tensor", convmm,
                       waits=[ev_xev, ev_diag[dsl]] + [b[1] for b in bc], sig="s_mm")
        ev_xbcs_free = ev_conv
        ev_diag_free[dsl] = ev_conv

        dest = (XF, BFm, CFm)[ot // 16]
        dt_ = ot % 16
        def cevac(e, bc=bc, dest=dest, dt_=dt_, ot=ot):
            e.activation(dest[:, dt_ * T: dt_ * T + 512], bc[0][0][:, :], AF.Identity,
                         bias=CONVB[:, ot:ot + 1])
            return e.activation(dest[:, dt_ * T + 512: dt_ * T + 1024], bc[1][0][:, :],
                                AF.Identity, bias=CONVB[:, ot:ot + 1])
        ev_ce = R.op("scalar", cevac, waits=[ev_conv], sig="s_sc")
        ring.rel(bc[0][2], ev_ce); ring.rel(bc[1][2], ev_ce)
        ev_conv_done[ot] = ev_ce
    ev_xf_last = ev_conv_done[15]
    ev_bf_last = ev_conv_done[31]
    ev_cf_last = ev_conv_done[47]
    ev_xbc_all = [ev_conv_done[i] for i in (15, 31, 47)]
    tap("a", AAR[0:32, :], [ev_a_ready])
    tap("xconv", XF[:, :], [ev_xf_last])
    tap("bconv", BFm[:, :], [ev_bf_last])
    tap("cconv", CFm[:, :], [ev_cf_last])
    if phases <= 3:
        return _finish()

    ev_zs_last = None
    for zi in range(16):
        sl = win_i % 3
        win_i += 1
        ev_win[sl] = R.op("gpsimd", ld(
            WINR[sl][:, :], d_wins[48 + zi].rearrange("p a b -> p (a b)")),
            waits=[ev_win_free[sl]], sig=wsem(), inc=16)
        bz = [ring.acq(), ring.acq()]

        def zmm(e, sl=sl, bz=bz):
            inst = None
            for (bi, off) in ((0, 3), (1, 515)):
                for kt in range(KT):
                    inst = e.matmul(bz[bi][0][:, :], WINR[sl][:, kt * 128:(kt + 1) * 128],
                                    XN[:, kt * TE + off: kt * TE + off + 512],
                                    start=(kt == 0), stop=(kt == KT - 1))
            return inst
        ev_zmm = R.op("tensor", zmm, waits=[ev_win[sl]] + [b[1] for b in bz], sig="s_mm")
        ev_win_free[sl] = ev_zmm

        def zev(e, zi=zi, bz=bz):
            e.activation(ZS[:, zi * T: zi * T + 512], bz[0][0][:, :], AF.Silu,
                         bias=COLS[:, 16 + zi:17 + zi])
            return e.activation(ZS[:, zi * T + 512: zi * T + 1024], bz[1][0][:, :],
                                AF.Silu, bias=COLS[:, 16 + zi:17 + zi])
        ev_zs_last = R.op("scalar", zev, waits=[ev_zmm], sig="s_sc")
        ring.rel(bz[0][2], ev_zs_last); ring.rel(bz[1][2], ev_zs_last)
    tap("zs", ZS[:, :], [ev_zs_last])
    if phases <= 4:
        return _finish()

    def chunk_prep(c, gate):
        evs = {}
        cw0 = c * 128
        evs["qch"] = R.op("vector", lambda e: e.tensor_tensor_scan(
            QCH[0:32, :], AAR[0:32, cw0:cw0 + 128],
            AAR[0:32, cw0:cw0 + 128], 0.0, ALU.add, ALU.bypass),
            waits=[ev_a_ready, gate], sig="s_ev")

        def dbs(e):
            e.activation(DBS[0:32, :], QCH[0:32, :], AF.Exp, scale=-1.0,
                         bias=QCH[0:32, 127:128])
            e.activation(ECS[0:32, :], QCH[0:32, :], AF.Exp)
            return e.activation(ACOL[0:32, :], QCH[0:32, 127:128], AF.Exp)
        evs["dbs"] = R.op("scalar", dbs, waits=[evs["qch"], gate], sig="s_sc")

        bq_, wq_, tq_ = ring.acq()
        def qtp(e, bq_=bq_):
            e.transpose(bq_[:, 0:32], QCH[0:32, :], IDF[0:32, 0:32])
            return e.transpose(bq_[:, 32:64], DBS[0:32, :], IDF[0:32, 0:32])
        evs["qtp"] = R.op("tensor", qtp, waits=[evs["qch"], evs["dbs"], wq_],
                          sig="s_mm")
        def qte(e, bq_=bq_):
            e.tensor_copy(QTC[:, :], bq_[:, 0:32])
            return e.tensor_copy(DBT[:, :], bq_[:, 32:64])
        evs["qte"] = R.op("vector", qte, waits=[evs["qtp"]], sig="s_ev")
        ring.rel(tq_, evs["qte"])

        ba_, wa_, ta_ = ring.acq()
        def amm(e, ba_=ba_):
            inst = None
            for pr in range(16):
                inst = e.matmul(ba_[:, pr:pr + 1], SELP[:, pr * 128:(pr + 1) * 128],
                                ACOL[0:32, :], start=True, stop=True)
            return inst
        evs["amm"] = R.op("tensor", amm, waits=[evs["dbs"], wa_, ev_const], sig="s_mm")
        evs["alc"] = R.op("vector", lambda e, ba_=ba_: e.tensor_copy(
            ALC[:, :], ba_[:, 0:16]), waits=[evs["amm"]], sig="s_ev")
        ring.rel(ta_, evs["alc"])

        def xtp(e):
            inst = None
            for pr in range(16):
                inst = e.transpose(PT[:, pr * 128:(pr + 1) * 128],
                                   XF[:, pr * T + cw0: pr * T + cw0 + 128], IDB[:, :])
            return inst
        evs["xtp"] = R.op("tensor", xtp, waits=[ev_xf_last, gate], sig="s_mm")
        evs["xtm"] = R.op("vector", lambda e: e.tensor_copy(XTM[:, :], PT[:, :]),
                          waits=[evs["xtp"], gate], sig="s_ev")

        def btp(e):
            inst = None
            for pr in range(16):
                inst = e.transpose(PT[:, pr * 128:(pr + 1) * 128],
                                   BFm[:, pr * T + cw0: pr * T + cw0 + 128], IDB[:, :])
            return inst
        evs["btp"] = R.op("tensor", btp, waits=[ev_bf_last, evs["xtm"]], sig="s_mm")

        def btev(e):
            inst = None
            for h in range(NH):
                pr, hh = h // 2, h % 2
                inst = e.tensor_scalar(
                    BTM[:, h * 64:(h + 1) * 64],
                    PT[:, pr * 128 + hh * 64: pr * 128 + hh * 64 + 64],
                    DBT[:, h:h + 1], None, ALU.mult)
            return inst
        evs["btm"] = R.op("vector", btev, waits=[evs["btp"], evs["qte"]], sig="s_ev")
        return evs

    def smm_update(evs, mode):
        bs = [ring.acq(), ring.acq()]
        def smm(e, bs=bs):
            inst = None
            for h in range(NH):
                pr, hh = h // 2, h % 2
                bb = bs[pr // 8][0]
                col = (pr % 8) * 64
                inst = e.matmul(bb[hh * 64:(hh + 1) * 64, col:col + 64],
                                BTM[:, h * 64:(h + 1) * 64], XTM[:, h * 64:(h + 1) * 64],
                                start=True, stop=True,
                                tile_position=(0, 64) if hh else None)
            return inst
        ev_smm = R.op("tensor", smm, waits=[evs["btm"], evs["xtm"]] + [b[1] for b in bs],
                      sig="s_mm")

        def hupd(e, bs=bs):
            inst = None
            if mode == "zero":
                e.tensor_copy(HST[:, 0:512], bs[0][0][:, :])
                inst = e.tensor_copy(HST[:, 512:1024], bs[1][0][:, :])
            else:
                for pr in range(16):
                    bb = bs[pr // 8][0]
                    col = (pr % 8) * 64
                    inst = e.scalar_tensor_tensor(
                        HST[:, pr * 64:(pr + 1) * 64], HST[:, pr * 64:(pr + 1) * 64],
                        ALC[:, pr:pr + 1], bb[:, col:col + 64], ALU.mult, ALU.add)
            return inst
        ev_h = R.op("vector", hupd, waits=[ev_smm, evs["alc"]], sig="s_ev")
        ring.rel(bs[0][2], ev_h); ring.rel(bs[1][2], ev_h)
        return ev_h

    gate = ev_zs_last
    for c in range(NC_):
        evs = chunk_prep(c, gate)
        gate = smm_update(evs, "zero" if c == 0 else "carry")
    ev_acore = R.op("vector", lambda e: e.tensor_reduce(
        ACOL[0:32, :], AAR[0:32, :], AX.X, ALU.add), waits=[ev_a_ready, gate],
        sig="s_ev")
    tap("score", HST[:, :], [gate])

    bt_, wt0_, tt0_ = ring.acq()
    ev_atp = R.op("tensor", lambda e, bt_=bt_: e.transpose(
        bt_[0:1, 0:32], ACOL[0:32, 0:1], IDF[0:32, 0:32]),
        waits=[ev_acore, wt0_], sig="s_mm")
    ev_arow = R.op("vector", lambda e, bt_=bt_: e.tensor_copy(
        WJS[0:1, 0:32], bt_[0:1, 0:32]), waits=[ev_atp], sig="s_ev")
    ring.rel(tt0_, ev_arow)
    ev_sdma = R.op("sync", lambda e: e.dma_start(d_sin[0:128, :], HST[:, :]),
                   waits=[gate], sig=usem(), inc=16)
    ev_adma = R.op("sync", lambda e: e.dma_start(d_sin[128:129, 0:32],
                                                 WJS[0:1, 0:32]),
                   waits=[ev_arow], sig=usem(), inc=16)
    ev_cc = R.op("gpsimd", lambda e: e.collective_compute(
        "AllGather", ALU.bypass, replica_groups=[list(range(NCORES))],
        ins=[d_sin.ap().opt()], outs=[d_sout.ap().opt()]),
        waits=[ev_sdma, ev_adma], sig="s_cc")
    ev_ag = R.op("sync", lambda e: e.dma_start(
        AGS[:, :], d_sout.ap().rearrange("(j r) c -> j r c", j=8)[:, 128, 0:32]),
        waits=[ev_cc], sig=usem(), inc=16)
    bw_, ww_, tw_ = ring.acq()
    ev_lw = R.op("tensor", lambda e, bw_=bw_: e.matmul(
        bw_[0:8, 0:32], CHAIN[:, :], AGS[0:8, :], start=True, stop=True),
        waits=[ev_ag, ww_, ev_const], sig="s_mm")
    ev_wj = R.op("scalar", lambda e, bw_=bw_: e.activation(
        WJS[0:8, :], bw_[0:8, 0:32], AF.Exp), waits=[ev_lw], sig="s_sc")
    ring.rel(tw_, ev_wj)
    ev_wjm = R.op("vector", lambda e: e.tensor_scalar(
        WJS[0:8, :], WJS[0:8, :], MKC[0:8, :], None, ALU.mult),
        waits=[ev_wj], sig="s_ev")
    bw2_, ww2_, tw2_ = ring.acq()
    ev_wt = R.op("tensor", lambda e, bw2_=bw2_: e.transpose(
        bw2_[0:32, 0:8], WJS[0:8, :], IDF[0:8, 0:8]), waits=[ev_wjm, ww2_], sig="s_mm")
    ev_wts = R.op("vector", lambda e, bw2_=bw2_: e.tensor_copy(
        WTS[0:32, :], bw2_[0:32, 0:8]), waits=[ev_wt], sig="s_ev")
    ring.rel(tw2_, ev_wts)
    bw3_, ww3_, tw3_ = ring.acq()
    def wcmm(e, bw3_=bw3_):
        inst = None
        for j in range(8):
            for pr in range(16):
                inst = e.matmul(bw3_[:, j * 16 + pr: j * 16 + pr + 1],
                                SELP[:, pr * 128:(pr + 1) * 128], WTS[0:32, j:j + 1],
                                start=True, stop=True)
        return inst
    ev_wc = R.op("tensor", wcmm, waits=[ev_wts, ww3_], sig="s_mm")
    ev_wcs = R.op("vector", lambda e, bw3_=bw3_: e.tensor_copy(
        WCOLS[:, :], bw3_[:, 0:128]), waits=[ev_wc], sig="s_ev")
    ring.rel(tw3_, ev_wcs)
    ev_hin = R.op("vector", lambda e: e.memset(HST[:, :], 0.0),
                  waits=[ev_sdma], sig="s_ev")
    ev_sjb_free = None
    for j in range(8):
        ev_sj = R.op("sync", lambda e, j=j: e.dma_start(
            SJB[:, :], d_sout[j * 129:j * 129 + 128, :]),
            waits=[ev_cc, ev_sjb_free], sig=usem(), inc=16)

        def hacc(e, j=j):
            inst = None
            for pr in range(16):
                inst = e.scalar_tensor_tensor(
                    HST[:, pr * 64:(pr + 1) * 64], SJB[:, pr * 64:(pr + 1) * 64],
                    WCOLS[:, j * 16 + pr: j * 16 + pr + 1],
                    HST[:, pr * 64:(pr + 1) * 64], ALU.mult, ALU.add)
            return inst
        ev_hin = R.op("vector", hacc, waits=[ev_sj, ev_wcs, ev_hin], sig="s_ev")
        ev_sjb_free = ev_hin
    ev_hbf = R.op("vector", lambda e: e.tensor_copy(HBF[:, :], HST[:, :]),
                  waits=[ev_hin], sig="s_ev")
    tap("hin", HST[:, :], [ev_hin])
    if phases <= 5:
        return _finish()

    ev_ygate_last = None
    ev_m2_prev = None
    ev_m3_prev = None
    mbuf_free = {0: None, 1: None}
    gate = ev_hbf
    snap("hbf2", HBF[:, :], [ev_hbf])
    for c in range(NC_):
        cw0 = c * 128
        evs = chunk_prep(c, gate)
        sev = []
        if c == 0:
            sev.append(snap("qch0", QCH[0:32, :], [evs["qch"]]))
            sev.append(snap("qtc0", QTC[:, :], [evs["qte"]]))
            sev.append(snap("dbs0", DBS[0:32, :], [evs["dbs"]]))
            sev.append(snap("ecs0", ECS[0:32, :], [evs["dbs"]]))
            sev.append(snap("alc0", ALC[:, :], [evs["alc"]]))
            sev.append(snap("xtm0", XTM[:, :], [evs["xtm"]]))
            sev.append(snap("btm0", BTM[:, :], [evs["btm"]]))
        if p2_lvl < 1:
            gate = evs["btm"]
            continue
        ev_ct = []
        for g4 in range(4):
            bc_, wc_, tc_ = ring.acq()
            def ecmm(e, g4=g4, bc_=bc_):
                inst = None
                for pp in range(4):
                    pr = g4 * 4 + pp
                    inst = e.matmul(bc_[:, pp * 128:(pp + 1) * 128],
                                    SELP[:, pr * 128:(pr + 1) * 128], ECS[0:32, :],
                                    start=True, stop=True)
                return inst
            ev_ecm = R.op("tensor", ecmm, waits=[evs["dbs"], wc_], sig="s_mm")

            def ctmul(e, g4=g4, bc_=bc_, cw0=cw0):
                src = CFm.ap().rearrange("p (pr t) -> p pr t", pr=16)[
                    :, g4 * 4:(g4 + 1) * 4, cw0:cw0 + 128]
                dst = CTLD.ap().rearrange("p (pr t) -> p pr t", pr=16)[
                    :, g4 * 4:(g4 + 1) * 4, 0:128]
                return e.tensor_tensor(dst, src,
                                       bc_.rearrange("p (pr t) -> p pr t", pr=4),
                                       ALU.mult)
            ev_c = R.op("vector", ctmul, waits=[ev_ecm, ev_cf_last, gate], sig="s_ev")
            ev_ct.append(ev_c)
            ring.rel(tc_, ev_c)
        if c == 0:
            sev.append(snap("ctld0", CTLD[:, :], [ev_ct[-1]]))
        if p2_lvl < 2:
            gate = evs["btm"]
            ev_ygate_last = ev_ct[-1]
            continue
        for g4 in range(8):
            h0 = g4 * 4
            bg0_, wg0_, tg0_ = ring.acq()
            bg1_, wg1_, tg1_ = ring.acq()
            bqr_, wqr_, tqr_ = ring.acq()
            bgs = (bg0_, bg1_)
            if p2_lvl < 20 and p2_lvl >= 10:
                def gmmX(e, h0=h0, bgs=bgs, cw0=cw0):
                    inst = None
                    for hh4 in range(4):
                        h = h0 + hh4
                        pr, hh = h // 2, h % 2
                        inst = e.matmul(
                            bgs[hh][:, (hh4 // 2) * 128:(hh4 // 2) * 128 + 128],
                            BFm[hh * 64:(hh + 1) * 64, pr * T + cw0: pr * T + cw0 + 128],
                            CFm[hh * 64:(hh + 1) * 64, pr * T + cw0: pr * T + cw0 + 128],
                            start=True, stop=True)
                    return inst
                ev_gm = R.op("tensor", gmmX, waits=[ev_bf_last, ev_cf_last, wg0_, wg1_],
                             sig="s_mm")
                evr = R.op("vector", lambda e, bgs=bgs: (
                    e.tensor_copy(ES2[:, 0:256], bgs[0][:, 0:256]),
                    e.tensor_copy(ES2[:, 256:512], bgs[1][:, 0:256]))[-1],
                    waits=[ev_gm], sig="s_ev")
                ring.rel(tg0_, evr); ring.rel(tg1_, evr)
                if p2_lvl >= 14:
                    def qrmmX(e, h0=h0, bqr_=bqr_):
                        inst = None
                        for hh4 in range(4):
                            sel = IDF[0:32, h0 + hh4:h0 + hh4 + 1].broadcast_to([32, 128])
                            inst = e.matmul(bqr_[:, hh4 * 128:(hh4 + 1) * 128], sel,
                                            QCH[0:32, :], start=True, stop=True)
                        return inst
                    ev_qr = R.op("tensor", qrmmX, waits=[evs["qch"], wqr_], sig="s_mm")
                    if p2_lvl >= 16:
                        def m1X(e, h0=h0, bqr_=bqr_):
                            inst = None
                            for hh4 in range(4):
                                inst = e.tensor_scalar(
                                    DS2[:, hh4 * 128:(hh4 + 1) * 128],
                                    bqr_[:, hh4 * 128:(hh4 + 1) * 128],
                                    QTC[:, h0 + hh4:h0 + hh4 + 1], None, ALU.subtract)
                            return inst
                        evq = R.op("vector", m1X, waits=[ev_qr, evs["qte"]], sig="s_ev")
                    else:
                        evq = R.op("vector", lambda e, bqr_=bqr_: e.tensor_copy(
                            DS2[:, :], bqr_[:, :]), waits=[ev_qr], sig="s_ev")
                    ring.rel(tqr_, evq)
                    ev_ygate_last = evq
                else:
                    ring.rel(tqr_, wqr_)
                    ev_ygate_last = evr
                continue

            def gmm(e, h0=h0, bgs=bgs, cw0=cw0):
                inst = None
                for hh4 in range(4):
                    h = h0 + hh4
                    pr, hh = h // 2, h % 2
                    tgt = bgs[hh][:, (hh4 // 2) * 128:(hh4 // 2) * 128 + 128]
                    e.matmul(tgt[0:1, 0:1], IDB[:, 0:1], IDB[:, 0:1],
                             start=True, stop=True)
                    inst = e.matmul(
                        tgt,
                        BFm[hh * 64:(hh + 1) * 64, pr * T + cw0: pr * T + cw0 + 128],
                        CFm[hh * 64:(hh + 1) * 64, pr * T + cw0: pr * T + cw0 + 128],
                        start=True, stop=True)
                return inst
            ev_gm = R.op("tensor", gmm, waits=[ev_bf_last, ev_cf_last, wg0_, wg1_],
                         sig="s_mm")
            if c == 0 and g4 == 0 and "bg00" in dbg_taps:
                ev_bgc = R.op("vector", lambda e, bg0_=bg0_: e.tensor_copy(
                    MTR[1][:, :].bitcast(F32)[:, 0:256], bg0_[:, 0:256]),
                    waits=[ev_gm, mbuf_free[1]], sig="s_ev")
                sv = snap("bg00", MTR[1][:, :].bitcast(F32)[:, 0:256], [ev_bgc])
                mbuf_free[1] = [sv]

            def qrmm(e, h0=h0, bqr_=bqr_):
                e.matmul(bqr_[0:1, 0:1], IDB[:, 0:1], IDB[:, 0:1],
                         start=True, stop=True)
                inst = None
                for hh4 in range(4):
                    sel = IDF[0:32, h0 + hh4:h0 + hh4 + 1].broadcast_to([32, 128])
                    inst = e.matmul(bqr_[:, hh4 * 128:(hh4 + 1) * 128], sel,
                                    QCH[0:32, :], start=True, stop=True)
                return inst
            ev_qr = R.op("tensor", qrmm, waits=[evs["qch"], wqr_], sig="s_mm")

            def m1(e, h0=h0, bqr_=bqr_):
                inst = None
                for hh4 in range(4):
                    inst = e.tensor_scalar(
                        DS2[:, hh4 * 128:(hh4 + 1) * 128],
                        bqr_[:, hh4 * 128:(hh4 + 1) * 128],
                        QTC[:, h0 + hh4:h0 + hh4 + 1], None, ALU.subtract)
                return inst
            ev_m1 = R.op("vector", m1, waits=[ev_qr, evs["qte"], ev_m2_prev],
                         sig="s_ev")
            ring.rel(tqr_, ev_m1)

            def m2(e):
                e.activation(ES2[:, :], DS2[:, :], AF.Relu, scale=-1.0)
                return e.activation(ES2[:, :], ES2[:, :], AF.Exp, scale=-1.0)
            ev_m2 = R.op("scalar", m2, waits=[ev_m1, ev_m3_prev], sig="s_sc")
            ev_m2_prev = ev_m2

            mbuf = MTR[g4 % 2]
            def m3(e, bgs=bgs, mbuf=mbuf):
                inst = None
                for hh4 in range(4):
                    sl_ = slice(hh4 * 128, (hh4 + 1) * 128)
                    e.tensor_tensor(ES2[:, sl_], ES2[:, sl_], CMASK[:, :], ALU.mult)
                    inst = e.tensor_tensor(
                        mbuf[:, sl_],
                        bgs[hh4 % 2][:, (hh4 // 2) * 128:(hh4 // 2) * 128 + 128],
                        ES2[:, sl_], ALU.mult)
                return inst
            ev_m3 = R.op("vector", m3, waits=[ev_m2, ev_gm, mbuf_free[g4 % 2]],
                         sig="s_ev")
            ev_m3_prev = ev_m3
            ring.rel(tg0_, ev_m3); ring.rel(tg1_, ev_m3)
            if c == 0 and g4 == 0:
                sv1 = snap("mt00", mbuf[:, :], [ev_m3])
                sv2 = snap("es00", ES2[:, :], [ev_m3])
                ev_m3_prev = [ev_m3, sv2]
                ev_m3 = [ev_m3, sv1]
            if p2_lvl < 3:
                mbuf_free[g4 % 2] = ev_m3
                ev_ygate_last = ev_m3
                continue

            by_, wy_, ty_ = ring.acq()
            def ymm(e, h0=h0, by_=by_, mbuf=mbuf):
                inst = None
                for hh4 in range(4):
                    h = h0 + hh4
                    pr, hh = h // 2, h % 2
                    pp = pr - h0 // 2
                    tgt = by_[hh * 64:(hh + 1) * 64, pp * 128:(pp + 1) * 128]
                    e.matmul(tgt, XTM[:, h * 64:(h + 1) * 64],
                             mbuf[:, hh4 * 128:(hh4 + 1) * 128],
                             start=True, stop=False,
                             tile_position=(0, 64) if hh else None)
                    inst = e.matmul(tgt,
                                    HBF[hh * 64:(hh + 1) * 64, pr * 64:(pr + 1) * 64],
                                    CTLD[hh * 64:(hh + 1) * 64, pr * 128:(pr + 1) * 128],
                                    start=False, stop=True,
                                    tile_position=(64, 64) if hh else None)
                return inst
            ev_ym = R.op("tensor", ymm,
                         waits=[ev_m3, evs["xtm"], ev_ct[g4 // 2], gate, wy_],
                         sig="s_mm")
            mbuf_free[g4 % 2] = ev_ym
            if p2_lvl < 4:
                ev_ev = R.op("vector", lambda e, by_=by_: e.tensor_copy(
                    ES2[:, 0:256], by_[:, 0:256]), waits=[ev_ym],
                    sig="s_ev")
                ring.rel(ty_, ev_ev)
                ev_ygate_last = ev_ev
                continue

            def gv(e, h0=h0, by_=by_, cw0=cw0):
                inst = None
                for pp in range(2):
                    pr = h0 // 2 + pp
                    xsl = XF[:, pr * T + cw0: pr * T + cw0 + 128]
                    psl = by_[:, pp * 128:(pp + 1) * 128]
                    e.scalar_tensor_tensor(xsl, xsl, COLS[:, 32 + pr:33 + pr], psl,
                                           ALU.mult, ALU.add)
                    inst = e.tensor_tensor(
                        xsl, xsl, ZS[:, pr * T + cw0: pr * T + cw0 + 128], ALU.mult)
                return inst
            ev_ge = R.op("vector", gv, waits=[ev_ym, ev_zs_last], sig="s_ev")
            ring.rel(ty_, ev_ge)
            ev_ygate_last = ev_ge
        if p2_lvl < 5:
            gate = [evs["btm"]] + sev
            continue
        ev_h2 = smm_update(evs, "carry")
        gate = R.op("vector", lambda e: e.tensor_copy(HBF[:, :], HST[:, :]),
                    waits=[ev_h2], sig="s_ev")
        gate = [gate] + sev
    tap("ygate", XF[:, :], [ev_ygate_last])
    if phases <= 6:
        return _finish()

    ev_uall = []
    for tt in range(8):
        ev_uall.append(R.op("sync", lambda e, tt=tt: e.dma_start(
            UALL[:, tt * D:(tt + 1) * D], d_u[3 + tt * 128: 3 + (tt + 1) * 128, :]),
            waits=[ev_ygate_last, gate], sig=usem(), inc=16))
    ev_wo = [None] * 3
    ev_wo_free = [ev_ygate_last] * 3
    ev_gb = [None, None]
    ev_gb_free = [ev_ygate_last, ev_ygate_last]
    opi_free = {i: None for i in range(4)}
    n_out = 0
    for ot in range(16):
        sl = ot % 3
        ev_wo[sl] = R.op("gpsimd", ld(
            WOR[sl][:, :], d_wouts[ot].rearrange("p a b -> p (a b)")),
            waits=[ev_wo_free[sl]], sig=wsem(), inc=16)
        gsl = ot % 2
        ev_gb[gsl] = R.op("sync", lambda e, ot=ot, gsl=gsl: e.dma_start(
            GBUF[gsl][:, :], d_g[ot * 128:(ot + 1) * 128, :]),
            waits=[ev_g_store[ot], ev_gb_free[gsl]], sig=usem(), inc=16)
        bo = [ring.acq(), ring.acq()]

        def womm(e, sl=sl, bo=bo):
            inst = None
            for (bi, off) in ((0, 0), (1, 512)):
                for kt in range(KT):
                    inst = e.matmul(bo[bi][0][:, :], WOR[sl][:, kt * 128:(kt + 1) * 128],
                                    XF[:, kt * T + off: kt * T + off + 512],
                                    start=(kt == 0), stop=(kt == KT - 1))
            return inst
        ev_wmm = R.op("tensor", womm,
                      waits=[ev_wo[sl], ev_ygate_last] + [b[1] for b in bo], sig="s_mm")
        ev_wo_free[sl] = ev_wmm

        def sgmul(e, bo=bo, gsl=gsl):
            e.tensor_tensor(SGT[:, 0:512], bo[0][0][:, :], GBUF[gsl][:, 0:512], ALU.mult)
            return e.tensor_tensor(SGT[:, 512:1024], bo[1][0][:, :],
                                   GBUF[gsl][:, 512:1024], ALU.mult)
        ev_sgm = R.op("vector", sgmul, waits=[ev_wmm, ev_gb[gsl]], sig="s_ev")
        ring.rel(bo[0][2], ev_sgm); ring.rel(bo[1][2], ev_sgm)
        ev_gb_free[gsl] = ev_sgm

        for tt in range(8):
            btx_, wtx_, ttx_ = ring.acq()
            ev_tx = R.op("tensor", lambda e, tt=tt, btx_=btx_: e.transpose(
                btx_[:, 0:128], SGT[:, tt * 128:(tt + 1) * 128], IDF[:, :]),
                waits=[ev_sgm, wtx_], sig="s_mm")
            oi_ = (ot * 8 + tt) % 4
            opi = OPI[oi_]
            ev_add = R.op("vector", lambda e, tt=tt, ot=ot, btx_=btx_, opi=opi:
                          e.tensor_tensor(opi[:, :], btx_[:, 0:128],
                                          UALL[:, tt * D + ot * 128: tt * D + (ot + 1) * 128],
                                          ALU.add),
                          waits=[ev_tx, ev_uall[tt], opi_free[oi_]], sig="s_ev")
            ring.rel(ttx_, ev_add)
            q_eng = "sync" if (ot * 8 + tt) % 2 == 0 else "scalar"
            ev_o = R.op(q_eng, lambda e, tt=tt, ot=ot, opi=opi: e.dma_start(
                d_out[tt * 128:(tt + 1) * 128, ot * 128:(ot + 1) * 128], opi[:, :]),
                waits=[ev_add], sig="s_out", inc=16)
            opi_free[oi_] = ev_o
            n_out += 1

    return _finish(n_out)


_SEMS = ["s_w0", "s_w1", "s_w2", "s_u0", "s_u1", "s_u2", "s_mm", "s_ev", "s_sc", "s_cc", "s_out", "s_gp"]


def prep_inputs(inputs):
    f32 = np.float32
    u = np.asarray(inputs["L0_attn_u"], f32)
    W_in = np.asarray(inputs["L0_attn_W_in"], f32)
    conv_w = np.asarray(inputs["L0_attn_conv_w"], f32)
    conv_b = np.asarray(inputs["L0_attn_conv_b"], f32)
    z_bias = np.asarray(inputs["L0_attn_z_bias"], f32)
    Dv = np.asarray(inputs["L0_attn_D"], f32)
    W_out = np.asarray(inputs["L0_attn_W_out"], f32)
    gate_w = np.asarray(inputs["L0_ffn_gate_w"], f32)
    sw1 = np.asarray(inputs["L0_ffn_sw1"], f32)
    sw2 = np.asarray(inputs["L0_ffn_sw2"], f32)
    sw3 = np.asarray(inputs["L0_ffn_sw3"], f32)
    rw1 = np.asarray(inputs["L0_ffn_rw1"], f32)
    rw2 = np.asarray(inputs["L0_ffn_rw2"], f32)
    rw3 = np.asarray(inputs["L0_ffn_rw3"], f32)

    C = np.ascontiguousarray

    W_pad = np.zeros((8320, D), f32)
    W_pad[:8224] = W_in
    wins = C(W_pad.reshape(65, 128, KT, 128).transpose(0, 3, 2, 1))
    wouts = C(W_out.reshape(16, 128, KT, 128).transpose(0, 3, 2, 1))

    w13 = np.empty((NE, FT, 128, 32, 128), f32)
    w2q = np.empty((NE, 4, 128, FT, 512), f32)
    for e in range(NE):
        if e < 2:
            w1f = sw1[:, e * 1408:(e + 1) * 1408]
            w3f = sw3[:, e * 1408:(e + 1) * 1408]
            w2f = sw2[e * 1408:(e + 1) * 1408, :]
        else:
            w1f, w3f, w2f = rw1[e - 2], rw3[e - 2], rw2[e - 2]
        w1b = w1f.reshape(KT, 128, FT, 128).transpose(2, 1, 0, 3)
        w3b = w3f.reshape(KT, 128, FT, 128).transpose(2, 1, 0, 3)
        w13[e, :, :, 0:16, :] = w1b
        w13[e, :, :, 16:32, :] = w3b
        w2q[e] = w2f.reshape(FT, 128, 4, 512).transpose(2, 1, 0, 3)
    w13 = C(w13)
    w2q = C(w2q)

    diag = np.zeros((48, 128, 4, 128), f32)
    cw3 = conv_w[:, 0, :].reshape(48, 128, 4)
    ii = np.arange(128)
    diag[:, ii, :, ii] = cw3.transpose(1, 0, 2)

    cols = np.zeros((128, 64), f32)
    cols[:, 0] = 1e-5
    cols[:, 1] = 1.0
    cols[:, 16:32] = z_bias.reshape(16, 128).T
    pr_idx = np.arange(16)
    pp_ = np.arange(128)
    cols[:, 32:48] = Dv[2 * pr_idx[None, :] + (pp_[:, None] // 64)]
    convb = C(conv_b.reshape(48, 128).T)
    gatew = C(gate_w.reshape(KT, 128, 8).transpose(1, 0, 2).reshape(128, 128))

    selp = np.zeros((32, 2048), f32)
    for pr in range(16):
        selp[2 * pr, pr * 128:pr * 128 + 64] = 1.0
        selp[2 * pr + 1, pr * 128 + 64:pr * 128 + 128] = 1.0
    cmask = np.triu(np.ones((128, 128), f32))
    idf = np.eye(128, dtype=f32)
    idb = np.eye(128, dtype=ml_dtypes.bfloat16)

    shared = {"wins": wins, "wouts": wouts, "w13": w13, "w2q": w2q, "diag": diag,
              "idf": idf, "idb": idb, "cmask": cmask, "cols": cols, "convb": convb,
              "gatew": gatew, "selp": selp}

    in_maps = []
    for core in range(NCORES):
        b, qq = core // 4, core % 4
        uext = np.zeros((TE, D), f32)
        t0 = qq * T
        if qq > 0:
            uext[0:3] = u[b, t0 - 3:t0]
        uext[3:] = u[b, t0:t0 + T]
        ch = np.zeros((8, 8), f32)
        mk = np.zeros((8, 1), f32)
        for j in range(4 * b, core):
            mk[j, 0] = 1.0
            for m in range(j + 1, core):
                ch[m, j] = 1.0
        m = dict(shared)
        m["uext"] = C(uext)
        m["chain"] = ch
        m["mkcol"] = mk
        in_maps.append(m)
    return in_maps


_CACHE = {}


def _run(inputs, dbg_taps=(), phases=99, p2_lvl=99):
    key = (tuple(sorted(dbg_taps)), phases, p2_lvl)
    if key not in _CACHE:
        nc, R = build_program(dbg_taps, phases, p2_lvl)
        sems = {}
        from contextlib import ExitStack
        with ExitStack() as st:
            for s in _SEMS:
                sems[s] = st.enter_context(nc.semaphore(s))
            R.emit(nc, sems)
        _CACHE[key] = nc
    nc = _CACHE[key]
    in_maps = prep_inputs(inputs)
    res = run_bass_kernel_spmd(nc, in_maps, core_ids=list(range(NCORES)))
    return res


def kernel(**inputs):
    res = _run(inputs)
    out = np.empty((2, 4096, D), np.float32)
    for core in range(NCORES):
        b, qq = core // 4, core % 4
        out[b, qq * T:(qq + 1) * T] = res.results[core]["out"]
    return out


# version 03
# speedup vs baseline: 1.0129x; 1.0129x over previous
import numpy as np
import ml_dtypes
import concourse.bass as bass
import concourse.mybir as mybir
from concourse.bass_utils import run_bass_kernel_spmd

F32 = mybir.dt.float32
BF16 = mybir.dt.bfloat16
AF = mybir.ActivationFunctionType
ALU = mybir.AluOpType
AX = mybir.AxisListType

T = 1024
TE = 1027
D = 2048
KT = 16
NH = 32
NC_ = 8
FT = 11
NE = 10
OT_A = 64
NCORES = 8

ENGINES = ("tensor", "vector", "scalar", "gpsimd", "sync")


class Rec:

    def __init__(self):
        self.ops = {e: [] for e in ENGINES}
        self.cnt = {}

    def op(self, eng, fn, waits=(), sig=None, inc=1):
        flat = []
        for w in waits:
            if w is None:
                continue
            if isinstance(w, list):
                flat.extend(x for x in w if x is not None)
            else:
                flat.append(w)
        waits = flat
        evt = None
        if sig is not None:
            self.cnt[sig] = self.cnt.get(sig, 0) + inc
            evt = (sig, self.cnt[sig])
        self.ops[eng].append((waits, fn, sig, inc))
        return evt

    def emit(self, nc, sems):
        hw = {}

        def runner(ename):
            def cb(eng):
                for waits, fn, sig, inc in self.ops[ename]:
                    for (s, v) in waits:
                        if hw.get((ename, s), -1) < v:
                            eng.wait_ge(sems[s], v)
                            hw[(ename, s)] = v
                    inst = fn(eng)
                    if sig is not None:
                        assert inst is not None, f"sig {sig} op returned None"
                        inst.then_inc(sems[sig], inc)
            return cb

        with nc.Block() as block:
            block.tensor(runner("tensor"))
            block.vector(runner("vector"))
            block.scalar(runner("scalar"))
            block.gpsimd(runner("gpsimd"))
            block.sync(runner("sync"))


class PsumRing:
    def __init__(self, views):
        self.views = views
        self.r = 0
        self.rel_ev = [None] * len(views)

    def acq(self):
        i = self.r % len(self.views)
        self.r += 1
        return self.views[i], self.rel_ev[i], i

    def rel(self, i, event):
        self.rel_ev[i] = event


def build_program(dbg_taps=(), phases=99, p2_lvl=99):
    nc = bass.Bass("TRN2", target_bir_lowering=False)
    R = Rec()

    def din(name, shape, dtype=F32):
        return nc.dram_tensor(name, list(shape), dtype, kind="ExternalInput")

    d_u = din("uext", [TE, D])
    d_wins = din("wins", [65, 128, KT, 128])
    d_wouts = din("wouts", [16, 128, KT, 128])
    d_w13 = din("w13", [NE, FT, 128, 32, 128])
    d_w2 = din("w2q", [NE, 4, 128, FT, 512])
    d_diag = din("diag", [48, 128, 4, 128])
    d_idf = din("idf", [128, 128])
    d_idb = din("idb", [128, 128], BF16)
    d_cmask = din("cmask", [128, 128])
    d_cols = din("cols", [128, 64])
    d_convb = din("convb", [128, 48])
    d_gatew = din("gatew", [128, 128])
    d_selp = din("selp", [32, 2048])
    d_chain = din("chain", [8, 8])
    d_mk = din("mkcol", [8, 1])

    d_out = nc.dram_tensor("out", [T, D], F32, kind="ExternalOutput")
    d_g = nc.dram_tensor("gscratch", [D, T], F32)
    d_sin = nc.dram_tensor("cc_s_in", [129, 1024], F32)
    d_sout = nc.dram_tensor("cc_s_out", [8 * 129, 1024], F32, addr_space="Shared")
    d_dbg = {}
    for name in dbg_taps:
        pass

    TOP = 229344

    def alloc(name, shape, dtype, at):
        size = int(np.prod(shape[1:])) * mybir.dt.size(dtype)
        assert at + size <= TOP, f"SBUF overflow {name}: {at}+{size}"
        return nc.alloc_sbuf_tensor_at(name, list(shape), dtype, offset=at)

    p = 20480
    def seq(name, shape, dtype):
        nonlocal p
        h = alloc(name, shape, dtype, p)
        p += ((int(np.prod(shape[1:])) * mybir.dt.size(dtype) + 31) // 32) * 32
        return h

    IDF = seq("s_idf", [128, 128], F32)
    IDB = seq("s_idb", [128, 128], BF16)
    CMASK = seq("s_cmask", [128, 128], F32)
    COLS = seq("s_cols", [128, 64], F32)
    CONVB = seq("s_convb", [128, 48], F32)
    GATEW = seq("s_gatew", [128, 128], BF16)
    SELP = seq("s_selp", [32, 2048], F32)
    CHAIN = seq("s_chain", [8, 8], F32)
    MKC = seq("s_mkcol", [8, 1], F32)
    XN = seq("s_xn", [128, KT * TE], BF16)
    GBASE = p
    assert GBASE <= 66000, GBASE

    Y = alloc("s_y", [128, 16 * T], F32, GBASE + 0)
    HE = alloc("s_he", [128, FT * T], BF16, GBASE + 65536)
    W2R = [alloc(f"s_w2r{i}", [128, FT * 512], BF16, GBASE + 88064 + i * 11264)
           for i in range(2)]
    W13R = [alloc(f"s_w13r{i}", [128, 4096], BF16, GBASE + 110592 + i * 8192)
            for i in range(2)]
    CWBC = alloc("s_cwbc", [128, T], BF16, GBASE + 126976)
    SP1 = alloc("s_sp1", [128, T], BF16, GBASE + 129024)
    P3C = alloc("s_p3c", [128, T], BF16, GBASE + 131072)
    SCF = alloc("s_scf", [8, T], F32, GBASE + 133120)
    SCT = alloc("s_sct", [128, 64], F32, GBASE + 137216)
    CWTM = alloc("s_cwtm", [128, 64], F32, GBASE + 137472)
    CWT = alloc("s_cwt", [8, T], BF16, GBASE + 137728)
    GCOL = alloc("s_gcol", [128, 16], F32, GBASE + 139776)
    UB = alloc("s_ub", [128, D], F32, GBASE + 139840)
    XTMP = alloc("s_xtmp", [128, D], BF16, GBASE + 148032)
    LNC = alloc("s_lnc", [128, 16], F32, GBASE + 152128)
    GTMP = [alloc(f"s_gtmp{i}", [128, T], F32, GBASE + 152192 + i * 4096)
            for i in range(2)]
    assert GBASE + 160384 <= TOP

    XF = alloc("s_xf", [128, 16 * T], BF16, GBASE + 0)
    BFm = alloc("s_bfm", [128, 16 * T], BF16, GBASE + 32768)
    CFm = alloc("s_cfm", [128, 16 * T], BF16, GBASE + 65536)
    ZS = alloc("s_zs", [128, 16 * T], BF16, GBASE + 98304)
    AAR = alloc("s_a", [32, T], F32, GBASE + 131072)
    HST = alloc("s_h", [128, T], F32, GBASE + 135168)
    HBF = alloc("s_hbf", [128, T], BF16, GBASE + 139264)
    MIXZ = GBASE + 141312
    WINR = [alloc(f"s_winr{i}", [128, 2048], BF16, MIXZ + i * 4096) for i in range(3)]
    DIAGR = [alloc(f"s_diagr{i}", [128, 512], BF16, MIXZ + 12288 + i * 1024)
             for i in range(2)]
    XBCS = [alloc("s_xbcs0", [128, 1040], BF16, MIXZ + 14336)]
    ASCR = alloc("s_ascr", [32, T], F32, MIXZ + 16416)
    XTM = alloc("s_xtm", [128, 2048], BF16, MIXZ + 0)
    BTM = alloc("s_btm", [128, 2048], BF16, MIXZ + 4096)
    CTLD = alloc("s_ctld", [128, 2048], BF16, MIXZ + 8192)
    SJB = alloc("s_sjb", [128, 1024], F32, MIXZ + 8192)
    MTR = [alloc(f"s_mtr{i}", [128, 512], BF16, MIXZ + 12288 + i * 1024)
           for i in range(2)]
    AGS = alloc("s_ags", [8, 32], F32, MIXZ + 12288)
    WJS = alloc("s_wjs", [8, 32], F32, MIXZ + 12416)
    WTS = alloc("s_wts", [32, 8], F32, MIXZ + 12544)
    WCOLS = alloc("s_wcols", [128, 128], F32, MIXZ + 12576)
    DS2 = alloc("s_ds2", [128, 512], F32, MIXZ + 14336)
    ES2 = alloc("s_es2", [128, 512], F32, MIXZ + 16384)
    QTC = alloc("s_qtc", [128, 32], F32, MIXZ + 18432)
    DBT = alloc("s_dbt", [128, 32], F32, MIXZ + 18560)
    ALC = alloc("s_alc", [128, 16], F32, MIXZ + 18688)
    QCH = alloc("s_qch", [32, 128], F32, MIXZ + 18752)
    DBS = alloc("s_dbs", [32, 128], F32, MIXZ + 19264)
    ECS = alloc("s_ecs", [32, 128], F32, MIXZ + 19776)
    ACOL = alloc("s_acol", [32, 1], F32, MIXZ + 20288)
    assert MIXZ + 20512 <= TOP
    UALL = alloc("s_uall", [128, 8 * D], F32, GBASE + 32768)
    SGT = alloc("s_sgt", [128, T], F32, GBASE + 98304)
    GBUF = [alloc(f"s_gbuf{i}", [128, T], F32, GBASE + 102400 + i * 4096)
            for i in range(2)]
    WOR = [alloc(f"s_wor{i}", [128, 2048], BF16, GBASE + 110592 + i * 4096)
           for i in range(3)]
    OPI = [alloc(f"s_opi{i}", [128, 128], F32, GBASE + 122880 + i * 512)
           for i in range(4)]

    PPA = nc.alloc_psum_tensor("ppa", [128, 1024], F32)
    PPB = nc.alloc_psum_tensor("ppb", [128, 1024], F32)
    PPC = nc.alloc_psum_tensor("ppc", [128, 1024], F32)
    PT = nc.alloc_psum_tensor("ppt", [128, 2048], BF16)
    ring = PsumRing([PPA[:, 0:512], PPA[:, 512:1024], PPB[:, 0:512],
                     PPB[:, 512:1024], PPC[:, 0:512], PPC[:, 512:1024]])

    taps = []

    def tap(name, ap, waits):
        if name in dbg_taps:
            dt = nc.dram_tensor("dbg_" + name, list(ap.shape), ap.dtype,
                                kind="ExternalOutput")
            taps.append((name, dt, ap, [w for w in waits if w is not None]))

    def snap(name, ap, waits):
        if name not in dbg_taps:
            return None
        dt = nc.dram_tensor("dbg_" + name, list(ap.shape), ap.dtype,
                            kind="ExternalOutput")
        return R.op("sync", lambda e, dt=dt, ap=ap: e.dma_start(
            dt[tuple(slice(0, s) for s in ap.shape)], ap),
            waits=waits, sig=usem(), inc=16)

    def ld(dst, src):
        return lambda e, d=dst, s=src: e.dma_start(d, s)

    _wc = [0]
    _uc = [0]

    def wsem():
        _wc[0] += 1
        return f"s_w{_wc[0] % 3}"

    def usem():
        _uc[0] += 1
        return f"s_u{_uc[0] % 3}"

    def _finish(n_out_=0):
        tap_done = []
        for name, dtt, ap, waits in taps:
            tap_done.append(R.op("sync", lambda e, dtt=dtt, ap=ap: e.dma_start(
                dtt[tuple(slice(0, s) for s in ap.shape)], ap), waits=waits,
                sig=usem(), inc=16))
        fw = [(s, R.cnt.get(s, 0)) for s in
              ("s_u0", "s_u1", "s_u2", "s_w0", "s_w1", "s_w2")]
        if n_out_:
            fw.append(("s_out", 16 * n_out_))
        R.op("sync", lambda e: None, waits=fw)
        return nc, R

    ev_const = []
    for dst, src in [(IDF[:, :], d_idf[:, :]), (IDB[:, :], d_idb[:, :]),
                     (CMASK[:, :], d_cmask[:, :]), (COLS[:, :], d_cols[:, :]),
                     (CONVB[:, :], d_convb[:, :]), (SELP[:, :], d_selp[:, :]),
                     (CHAIN[:, :], d_chain[:, :]), (MKC[:, :], d_mk[:, :])]:
        ev_const.append(R.op("sync", ld(dst, src), sig=usem(), inc=16))
    ev_gatew = R.op("gpsimd", ld(GATEW[:, :], d_gatew[:, :]), sig=wsem(), inc=16)

    tiles = [(i * 128, min(TE, (i + 1) * 128)) for i in range(9)]
    ev_ubuf_free = None
    ev_pt_free = None
    ev_xtmp_free = None
    ev_rstd_prev = None
    ev_xn = None
    for ti, (r0, r1) in enumerate(tiles):
        P = r1 - r0
        ub = UB
        ev_u = R.op("sync", ld(ub[0:P, :], d_u[r0:r1, :]),
                    waits=[ev_ubuf_free], sig=usem(), inc=16)

        ev_red = R.op("vector", lambda e, ub=ub, P=P: e.tensor_reduce(
            LNC[0:P, 0:1], ub[0:P, :], AX.X, ALU.add, negate=True),
            waits=[ev_u, ev_rstd_prev], sig="s_ev")
        ev_mg = R.op("gpsimd", lambda e, P=P: e.tensor_scalar(
            LNC[0:P, 1:2], LNC[0:P, 0:1], 1.0 / D, None, ALU.mult),
            waits=[ev_red], sig="s_gp")
        ev_ms = R.op("vector", lambda e, ub=ub, P=P: e.tensor_scalar(
            ub[0:P, :], ub[0:P, :], LNC[0:P, 1:2], None, ALU.add),
            waits=[ev_mg], sig="s_ev")
        ev_sq = R.op("scalar", lambda e, ub=ub, P=P: e.activation(
            XTMP[0:P, :], ub[0:P, :], AF.Square),
            waits=[ev_ms, ev_xtmp_free], sig="s_sc")
        ev_s2 = R.op("vector", lambda e, P=P: e.tensor_reduce(
            LNC[0:P, 2:3], XTMP[0:P, :], AX.X, ALU.add), waits=[ev_sq], sig="s_ev")

        def rstd(e, P=P):
            e.activation(LNC[0:P, 3:4], LNC[0:P, 2:3], AF.Ln, scale=1.0 / D,
                         bias=COLS[0:P, 0:1])
            return e.activation(LNC[0:P, 4:5], LNC[0:P, 3:4], AF.Exp, scale=-0.5)
        ev_rstd = R.op("scalar", rstd, waits=[ev_s2, ev_const], sig="s_sc")
        ev_rstd_prev = ev_rstd
        ev_xt = R.op("vector", lambda e, ub=ub, P=P: e.tensor_scalar(
            XTMP[0:P, :], ub[0:P, :], LNC[0:P, 4:5], None,
            ALU.mult), waits=[ev_rstd], sig="s_ev")
        ev_ubuf_free = ev_xt

        def transposes(e, P=P):
            inst = None
            for kt in range(KT):
                inst = e.transpose(PT[:, kt * 128:kt * 128 + P],
                                   XTMP[0:P, kt * 128:(kt + 1) * 128], IDB[0:P, 0:P])
            return inst
        ev_tp = R.op("tensor", transposes, waits=[ev_xt, ev_pt_free, ev_const],
                     sig="s_mm")
        ev_xtmp_free = ev_tp

        def evac(e, r0=r0, P=P):
            src = PT.ap().rearrange("p (kt c) -> p kt c", kt=KT)[:, :, 0:P]
            dst = XN.ap().rearrange("p (kt c) -> p kt c", kt=KT)[:, :, r0:r0 + P]
            return e.tensor_copy(dst, src)
        ev_xn = R.op("vector", evac, waits=[ev_tp], sig="s_ev")
        ev_pt_free = ev_xn
    tap("xn", XN[:, :], [ev_xn])
    if phases <= 1:
        return _finish()

    b0, w0, t0 = ring.acq()
    b1, w1, t1 = ring.acq()

    def gate_mm(e, b0=b0, b1=b1):
        inst = None
        for nch, bb in ((0, b0), (1, b1)):
            for kt in range(KT):
                inst = e.matmul(bb[0:8, :], GATEW[:, kt * 8:(kt + 1) * 8],
                                XN[:, kt * TE + 3 + nch * 512: kt * TE + 3 + nch * 512 + 512],
                                start=(kt == 0), stop=(kt == KT - 1))
        return inst
    ev_gmm = R.op("tensor", gate_mm, waits=[ev_xn, ev_gatew, w0, w1], sig="s_mm")

    def gevac(e, b0=b0, b1=b1):
        e.tensor_copy(SCF[:, 0:512], b0[0:8, :])
        return e.tensor_copy(SCF[:, 512:1024], b1[0:8, :])
    ev_gse = R.op("vector", gevac, waits=[ev_gmm], sig="s_ev")
    ring.rel(t0, ev_gse); ring.rel(t1, ev_gse)

    b0, w0, t0 = ring.acq()
    def gtp(e, b0=b0):
        inst = None
        for tt in range(8):
            inst = e.transpose(b0[:, tt * 8:(tt + 1) * 8],
                               SCF[0:8, tt * 128:(tt + 1) * 128], IDF[0:8, 0:8])
        return inst
    ev_gtp = R.op("tensor", gtp, waits=[ev_gse, w0], sig="s_mm")
    ev_sct = R.op("vector", lambda e, b0=b0: e.tensor_copy(SCT[:, :], b0[:, 0:64]),
                  waits=[ev_gtp], sig="s_ev")
    ring.rel(t0, ev_sct)

    def smax_a(e):
        inst = None
        for tt in range(8):
            inst = e.tensor_reduce(GCOL[:, tt:tt + 1], SCT[:, tt * 8:(tt + 1) * 8],
                                   AX.X, ALU.max, negate=True)
        return inst
    ev_nm = R.op("vector", smax_a, waits=[ev_sct], sig="s_ev")

    def smax_b(e):
        inst = None
        for tt in range(8):
            inst = e.activation(SCT[:, tt * 8:(tt + 1) * 8], SCT[:, tt * 8:(tt + 1) * 8],
                                AF.Exp, bias=GCOL[:, tt:tt + 1],
                                accum_out=GCOL[:, 8 + tt:9 + tt])
        return inst
    ev_ex = R.op("scalar", smax_b, waits=[ev_nm], sig="s_sc")

    S8 = lambda tt: SCT[:, tt * 8:(tt + 1) * 8]
    M8 = lambda tt: CWTM[:, tt * 8:(tt + 1) * 8]

    def stepf(fn):
        def g(e):
            inst = None
            for tt in range(8):
                inst = fn(e, tt)
            return inst
        return g
    ev = R.op("vector", stepf(lambda e, tt: e.reciprocal(
        GCOL[:, tt:tt + 1], GCOL[:, 8 + tt:9 + tt])), waits=[ev_ex], sig="s_ev")
    ev = R.op("gpsimd", stepf(lambda e, tt: e.tensor_scalar(
        S8(tt), S8(tt), GCOL[:, tt:tt + 1], None, ALU.mult)), waits=[ev], sig="s_gp")
    ev = R.op("vector", stepf(lambda e, tt: e.tensor_reduce(
        GCOL[:, tt:tt + 1], S8(tt), AX.X, ALU.max)), waits=[ev], sig="s_ev")
    ev = R.op("gpsimd", stepf(lambda e, tt: e.tensor_scalar(
        M8(tt), S8(tt), GCOL[:, tt:tt + 1], None, ALU.is_ge)), waits=[ev], sig="s_gp")
    ev = R.op("vector", stepf(lambda e, tt: e.scalar_tensor_tensor(
        M8(tt), M8(tt), -2.0, S8(tt), ALU.mult, ALU.add)), waits=[ev], sig="s_ev")
    ev = R.op("vector", stepf(lambda e, tt: e.tensor_reduce(
        GCOL[:, tt:tt + 1], M8(tt), AX.X, ALU.max)), waits=[ev], sig="s_ev")
    ev = R.op("gpsimd", stepf(lambda e, tt: e.tensor_scalar(
        M8(tt), S8(tt), GCOL[:, tt:tt + 1], None, ALU.is_ge)), waits=[ev], sig="s_gp")
    ev_cw = R.op("vector", stepf(lambda e, tt: e.tensor_tensor(
        S8(tt), S8(tt), M8(tt), ALU.mult)), waits=[ev], sig="s_ev")
    tap("cw", SCT[:, :], [ev_cw])

    b0, w0, t0 = ring.acq()
    b1, w1, t1 = ring.acq()
    def cwtp(e, b0=b0, b1=b1):
        inst = None
        for tt in range(8):
            bb = b0 if tt < 4 else b1
            inst = e.transpose(bb[0:8, (tt % 4) * 128:(tt % 4 + 1) * 128],
                               SCT[:, tt * 8:(tt + 1) * 8], IDF[:, :])
        return inst
    ev_cwtp = R.op("tensor", cwtp, waits=[ev_cw, w0, w1], sig="s_mm")

    def cwte(e, b0=b0, b1=b1):
        e.tensor_copy(CWT[:, 0:512], b0[0:8, :])
        return e.tensor_copy(CWT[:, 512:1024], b1[0:8, :])
    ev_cwt = R.op("vector", cwte, waits=[ev_cwtp], sig="s_ev")
    ring.rel(t0, ev_cwt); ring.rel(t1, ev_cwt)

    ev_w13 = [None] * 2
    ev_w13_free = [None] * 2
    ev_w2 = [None, None]
    ev_w2_free = [None, None]
    ev_y = None
    ev_pe_prev = None
    ev_he_prev = None
    slot_i = 0
    for e_i in range(NE):
        routed = e_i >= 2
        ev_cwbc = None
        if routed:
            re = e_i - 2
            bc0, wc0, tc0 = ring.acq()
            bc1, wc1, tc1 = ring.acq()

            def cwbc_mm(e, re=re, bc0=bc0, bc1=bc1):
                sel = IDB[0:8, re:re + 1].broadcast_to([8, 128])
                e.matmul(bc0[:, :], sel, CWT[0:8, 0:512], start=True, stop=True)
                return e.matmul(bc1[:, :], sel, CWT[0:8, 512:1024], start=True, stop=True)
            ev_bc = R.op("tensor", cwbc_mm, waits=[ev_cwt, wc0, wc1], sig="s_mm")

            def cwbc_ev(e, bc0=bc0, bc1=bc1):
                e.tensor_copy(CWBC[:, 0:512], bc0[:, :])
                return e.tensor_copy(CWBC[:, 512:1024], bc1[:, :])
            ev_cwbc = R.op("vector", cwbc_ev, waits=[ev_bc], sig="s_ev")
            ring.rel(tc0, ev_cwbc); ring.rel(tc1, ev_cwbc)

        ev_he_all = []
        for ft in range(FT):
            sl = slot_i % 2
            slot_i += 1
            ev_w13[sl] = R.op("gpsimd", ld(
                W13R[sl][:, :], d_w13[e_i, ft].rearrange("p a b -> p (a b)")),
                waits=[ev_w13_free[sl]], sig=wsem(), inc=16)
            bq = [ring.acq() for _ in range(4)]

            def pmm(e, sl=sl, bq=bq):
                inst = None
                for half in range(2):
                    off = 3 + half * 512
                    for kt in range(KT):
                        e.matmul(bq[half][0][:, :], W13R[sl][:, kt * 128:(kt + 1) * 128],
                                 XN[:, kt * TE + off: kt * TE + off + 512],
                                 start=(kt == 0), stop=(kt == KT - 1))
                    for kt in range(KT):
                        inst = e.matmul(bq[2 + half][0][:, :],
                                        W13R[sl][:, 2048 + kt * 128: 2048 + (kt + 1) * 128],
                                        XN[:, kt * TE + off: kt * TE + off + 512],
                                        start=(kt == 0), stop=(kt == KT - 1))
                return inst
            ev_pmm = R.op("tensor", pmm,
                          waits=[ev_w13[sl], ev_xn] + [b[1] for b in bq], sig="s_mm")
            ev_w13_free[sl] = ev_pmm

            def pevac(e, bq=bq):
                e.activation(SP1[:, 0:512], bq[0][0][:, :], AF.Silu)
                e.activation(SP1[:, 512:1024], bq[1][0][:, :], AF.Silu)
                e.activation(P3C[:, 0:512], bq[2][0][:, :], AF.Copy)
                return e.activation(P3C[:, 512:1024], bq[3][0][:, :], AF.Copy)
            ev_pe = R.op("scalar", pevac, waits=[ev_pmm, ev_he_prev], sig="s_sc")
            ev_pe_prev = ev_pe
            for b in bq:
                ring.rel(b[2], ev_pe)

            if routed:
                def hmul(e, ft=ft):
                    e.tensor_tensor(HE[:, ft * T:(ft + 1) * T], SP1[:, :], P3C[:, :],
                                    ALU.mult)
                    return e.tensor_tensor(HE[:, ft * T:(ft + 1) * T],
                                           HE[:, ft * T:(ft + 1) * T], CWBC[:, :],
                                           ALU.mult)
                ev_he = R.op("vector", hmul, waits=[ev_pe, ev_cwbc], sig="s_ev")
            else:
                def hmul(e, ft=ft):
                    return e.tensor_tensor(HE[:, ft * T:(ft + 1) * T], SP1[:, :],
                                           P3C[:, :], ALU.mult)
                ev_he = R.op("vector", hmul, waits=[ev_pe], sig="s_ev")
            ev_he_prev = ev_he
            ev_he_all.append(ev_he)

        for oq in range(4):
            sl2 = oq % 2
            ev_w2[sl2] = R.op("gpsimd", ld(
                W2R[sl2][:, :], d_w2[e_i, oq].rearrange("p a b -> p (a b)")),
                waits=[ev_w2_free[sl2]], sig=wsem(), inc=16)
            for tq in range(4):
                bk = [ring.acq(), ring.acq()]

                def w2mm(e, sl2=sl2, tq=tq, bk=bk):
                    inst = None
                    for osub in range(4):
                        tgt = bk[osub // 2][0][:, (osub % 2) * 256:(osub % 2) * 256 + 256]
                        for ft in range(FT):
                            inst = e.matmul(
                                tgt,
                                W2R[sl2][:, ft * 512 + osub * 128: ft * 512 + (osub + 1) * 128],
                                HE[:, ft * T + tq * 256: ft * T + tq * 256 + 256],
                                start=(ft == 0), stop=(ft == FT - 1))
                    return inst
                ev_wmm = R.op("tensor", w2mm,
                              waits=[ev_w2[sl2], ev_he_all[-1]] + [b[1] for b in bk],
                              sig="s_mm")
                if tq == 3:
                    ev_w2_free[sl2] = ev_wmm

                def yacc(e, oq=oq, tq=tq, bk=bk, first=(e_i == 0)):
                    inst = None
                    for half in range(2):
                        srcv = bk[half][0].rearrange("p (o t) -> p o t", o=2)
                        dst = Y.ap().rearrange("p (o t) -> p o t", o=16)[
                            :, 4 * oq + 2 * half: 4 * oq + 2 * half + 2,
                            tq * 256: tq * 256 + 256]
                        if first:
                            inst = e.tensor_copy(dst, srcv)
                        else:
                            inst = e.scalar_tensor_tensor(dst, srcv, 1.0, dst,
                                                          ALU.mult, ALU.add)
                    return inst
                ev_y = R.op("vector", yacc, waits=[ev_wmm], sig="s_ev")
                for b in bk:
                    ring.rel(b[2], ev_y)
    tap("yffn", Y[:, :], [ev_y])

    ev_g_store = []
    ev_gt_free = [None, None]
    for ot in range(16):
        gb = GTMP[ot % 2]
        ev_sg = R.op("scalar", lambda e, ot=ot, gb=gb: e.activation(
            gb[:, :], Y[:, ot * T:(ot + 1) * T], AF.Silu),
            waits=[ev_y, ev_gt_free[ot % 2]], sig="s_sc")
        ev_st = R.op("sync", lambda e, ot=ot, gb=gb: e.dma_start(
            d_g[ot * 128:(ot + 1) * 128, :], gb[:, :]), waits=[ev_sg], sig=usem(), inc=16)
        ev_gt_free[ot % 2] = ev_st
        ev_g_store.append(ev_st)
    if phases <= 2:
        return _finish()

    gate_mix = ev_g_store[-1]
    ev_win = [None] * 3
    ev_win_free = [gate_mix] * 3
    ev_diag = [None] * 2
    ev_diag_free = [gate_mix] * 2
    ev_xbcs_free = gate_mix
    ev_conv_done = {}
    ev_a_ready = None
    win_i = 0
    for oi, ot in enumerate(list(range(48)) + [OT_A]):
        sl = win_i % 3
        win_i += 1
        ev_win[sl] = R.op("gpsimd", ld(
            WINR[sl][:, :], d_wins[ot].rearrange("p a b -> p (a b)")),
            waits=[ev_win_free[sl]], sig=wsem(), inc=16)
        is_a = (ot == OT_A)
        if not is_a:
            dsl = oi % 2
            ev_diag[dsl] = R.op("gpsimd", ld(
                DIAGR[dsl][:, :], d_diag[ot].rearrange("p a b -> p (a b)")),
                waits=[ev_diag_free[dsl]], sig=wsem(), inc=16)
        bm = [ring.acq(), ring.acq(), ring.acq()]

        def projmm(e, sl=sl, bm=bm):
            inst = None
            for (bi, off, n) in ((0, 0, 512), (1, 512, 512), (2, 1024, 3)):
                for kt in range(KT):
                    inst = e.matmul(bm[bi][0][:, 0:n],
                                    WINR[sl][:, kt * 128:(kt + 1) * 128],
                                    XN[:, kt * TE + off: kt * TE + off + n],
                                    start=(kt == 0), stop=(kt == KT - 1))
            return inst
        ev_proj = R.op("tensor", projmm,
                       waits=[ev_win[sl], ev_xn] + [b[1] for b in bm], sig="s_mm")
        ev_win_free[sl] = ev_proj

        if is_a:
            def aev(e, bm=bm):
                e.tensor_copy(ASCR[0:32, 0:509], bm[0][0][0:32, 3:512])
                e.tensor_copy(ASCR[0:32, 509:1021], bm[1][0][0:32, :])
                return e.tensor_copy(ASCR[0:32, 1021:1024], bm[2][0][0:32, 0:3])
            ev_acp = R.op("vector", aev, waits=[ev_proj, gate_mix], sig="s_ev")
            for b in bm:
                ring.rel(b[2], ev_acp)

            def soft1(e):
                e.activation(AAR[0:32, :], ASCR[0:32, :], AF.Abs)
                e.activation(AAR[0:32, :], AAR[0:32, :], AF.Exp, scale=-1.0)
                return e.activation(AAR[0:32, :], AAR[0:32, :], AF.Ln,
                                    bias=COLS[0:32, 1:2])
            ev_sp = R.op("scalar", soft1, waits=[ev_acp, gate_mix], sig="s_sc")

            def soft2(e):
                e.tensor_scalar(ASCR[0:32, :], ASCR[0:32, :], 0.0, None, ALU.max)
                return e.scalar_tensor_tensor(AAR[0:32, :], AAR[0:32, :], -1.0,
                                              ASCR[0:32, :], ALU.mult, ALU.subtract)
            ev_a_ready = R.op("vector", soft2, waits=[ev_sp], sig="s_ev")
            continue

        xb = XBCS[0]
        def pevac2(e, bm=bm, xb=xb):
            e.tensor_copy(xb[:, 0:512], bm[0][0][:, :])
            e.tensor_copy(xb[:, 512:1024], bm[1][0][:, :])
            return e.tensor_copy(xb[:, 1024:1027], bm[2][0][:, 0:3])
        ev_xev = R.op("vector", pevac2, waits=[ev_proj, ev_xbcs_free],
                      sig="s_ev")
        for b in bm:
            ring.rel(b[2], ev_xev)

        bc = [ring.acq(), ring.acq()]
        dsl = oi % 2
        def convmm(e, xb=xb, dsl=dsl, bc=bc):
            inst = None
            for (bi, c0) in ((0, 0), (1, 512)):
                for k in range(4):
                    inst = e.matmul(bc[bi][0][:, :], DIAGR[dsl][:, k * 128:(k + 1) * 128],
                                    xb[:, k + c0: k + c0 + 512],
                                    start=(k == 0), stop=(k == 3))
            return inst
        ev_conv = R.op("tensor", convmm,
                       waits=[ev_xev, ev_diag[dsl]] + [b[1] for b in bc], sig="s_mm")
        ev_xbcs_free = ev_conv
        ev_diag_free[dsl] = ev_conv

        dest = (XF, BFm, CFm)[ot // 16]
        dt_ = ot % 16
        def cevac(e, bc=bc, dest=dest, dt_=dt_, ot=ot):
            e.activation(dest[:, dt_ * T: dt_ * T + 512], bc[0][0][:, :], AF.Identity,
                         bias=CONVB[:, ot:ot + 1])
            return e.activation(dest[:, dt_ * T + 512: dt_ * T + 1024], bc[1][0][:, :],
                                AF.Identity, bias=CONVB[:, ot:ot + 1])
        ev_ce = R.op("scalar", cevac, waits=[ev_conv], sig="s_sc")
        ring.rel(bc[0][2], ev_ce); ring.rel(bc[1][2], ev_ce)
        ev_conv_done[ot] = ev_ce
    ev_xf_last = ev_conv_done[15]
    ev_bf_last = ev_conv_done[31]
    ev_cf_last = ev_conv_done[47]
    ev_xbc_all = [ev_conv_done[i] for i in (15, 31, 47)]
    tap("a", AAR[0:32, :], [ev_a_ready])
    tap("xconv", XF[:, :], [ev_xf_last])
    tap("bconv", BFm[:, :], [ev_bf_last])
    tap("cconv", CFm[:, :], [ev_cf_last])
    if phases <= 3:
        return _finish()

    ev_zs_last = None
    for zi in range(16):
        sl = win_i % 3
        win_i += 1
        ev_win[sl] = R.op("gpsimd", ld(
            WINR[sl][:, :], d_wins[48 + zi].rearrange("p a b -> p (a b)")),
            waits=[ev_win_free[sl]], sig=wsem(), inc=16)
        bz = [ring.acq(), ring.acq()]

        def zmm(e, sl=sl, bz=bz):
            inst = None
            for (bi, off) in ((0, 3), (1, 515)):
                for kt in range(KT):
                    inst = e.matmul(bz[bi][0][:, :], WINR[sl][:, kt * 128:(kt + 1) * 128],
                                    XN[:, kt * TE + off: kt * TE + off + 512],
                                    start=(kt == 0), stop=(kt == KT - 1))
            return inst
        ev_zmm = R.op("tensor", zmm, waits=[ev_win[sl]] + [b[1] for b in bz], sig="s_mm")
        ev_win_free[sl] = ev_zmm

        def zev(e, zi=zi, bz=bz):
            e.activation(ZS[:, zi * T: zi * T + 512], bz[0][0][:, :], AF.Silu,
                         bias=COLS[:, 16 + zi:17 + zi])
            return e.activation(ZS[:, zi * T + 512: zi * T + 1024], bz[1][0][:, :],
                                AF.Silu, bias=COLS[:, 16 + zi:17 + zi])
        ev_zs_last = R.op("scalar", zev, waits=[ev_zmm], sig="s_sc")
        ring.rel(bz[0][2], ev_zs_last); ring.rel(bz[1][2], ev_zs_last)
    tap("zs", ZS[:, :], [ev_zs_last])
    if phases <= 4:
        return _finish()

    def chunk_prep(c, gate):
        evs = {}
        cw0 = c * 128
        evs["qch"] = R.op("vector", lambda e: e.tensor_tensor_scan(
            QCH[0:32, :], AAR[0:32, cw0:cw0 + 128],
            AAR[0:32, cw0:cw0 + 128], 0.0, ALU.add, ALU.bypass),
            waits=[ev_a_ready, gate], sig="s_ev")

        def dbs(e):
            e.activation(DBS[0:32, :], QCH[0:32, :], AF.Exp, scale=-1.0,
                         bias=QCH[0:32, 127:128])
            e.activation(ECS[0:32, :], QCH[0:32, :], AF.Exp)
            return e.activation(ACOL[0:32, :], QCH[0:32, 127:128], AF.Exp)
        evs["dbs"] = R.op("scalar", dbs, waits=[evs["qch"], gate], sig="s_sc")

        bq_, wq_, tq_ = ring.acq()
        def qtp(e, bq_=bq_):
            e.transpose(bq_[:, 0:32], QCH[0:32, :], IDF[0:32, 0:32])
            return e.transpose(bq_[:, 32:64], DBS[0:32, :], IDF[0:32, 0:32])
        evs["qtp"] = R.op("tensor", qtp, waits=[evs["qch"], evs["dbs"], wq_],
                          sig="s_mm")
        def qte(e, bq_=bq_):
            e.tensor_copy(QTC[:, :], bq_[:, 0:32])
            return e.tensor_copy(DBT[:, :], bq_[:, 32:64])
        evs["qte"] = R.op("vector", qte, waits=[evs["qtp"]], sig="s_ev")
        ring.rel(tq_, evs["qte"])

        ba_, wa_, ta_ = ring.acq()
        def amm(e, ba_=ba_):
            inst = None
            for pr in range(16):
                inst = e.matmul(ba_[:, pr:pr + 1], SELP[:, pr * 128:(pr + 1) * 128],
                                ACOL[0:32, :], start=True, stop=True)
            return inst
        evs["amm"] = R.op("tensor", amm, waits=[evs["dbs"], wa_, ev_const], sig="s_mm")
        evs["alc"] = R.op("vector", lambda e, ba_=ba_: e.tensor_copy(
            ALC[:, :], ba_[:, 0:16]), waits=[evs["amm"]], sig="s_ev")
        ring.rel(ta_, evs["alc"])

        def xtp(e):
            inst = None
            for pr in range(16):
                inst = e.transpose(PT[:, pr * 128:(pr + 1) * 128],
                                   XF[:, pr * T + cw0: pr * T + cw0 + 128], IDB[:, :])
            return inst
        evs["xtp"] = R.op("tensor", xtp, waits=[ev_xf_last, gate], sig="s_mm")
        evs["xtm"] = R.op("vector", lambda e: e.tensor_copy(XTM[:, :], PT[:, :]),
                          waits=[evs["xtp"], gate], sig="s_ev")

        def btp(e):
            inst = None
            for pr in range(16):
                inst = e.transpose(PT[:, pr * 128:(pr + 1) * 128],
                                   BFm[:, pr * T + cw0: pr * T + cw0 + 128], IDB[:, :])
            return inst
        evs["btp"] = R.op("tensor", btp, waits=[ev_bf_last, evs["xtm"]], sig="s_mm")

        def btev(e):
            inst = None
            for h in range(NH):
                pr, hh = h // 2, h % 2
                inst = e.tensor_scalar(
                    BTM[:, h * 64:(h + 1) * 64],
                    PT[:, pr * 128 + hh * 64: pr * 128 + hh * 64 + 64],
                    DBT[:, h:h + 1], None, ALU.mult)
            return inst
        evs["btm"] = R.op("vector", btev, waits=[evs["btp"], evs["qte"]], sig="s_ev")
        return evs

    def smm_update(evs, mode):
        bs = [ring.acq(), ring.acq()]
        def smm(e, bs=bs):
            inst = None
            for h in range(NH):
                pr, hh = h // 2, h % 2
                bb = bs[pr // 8][0]
                col = (pr % 8) * 64
                inst = e.matmul(bb[hh * 64:(hh + 1) * 64, col:col + 64],
                                BTM[:, h * 64:(h + 1) * 64], XTM[:, h * 64:(h + 1) * 64],
                                start=True, stop=True,
                                tile_position=(0, 64) if hh else None)
            return inst
        ev_smm = R.op("tensor", smm, waits=[evs["btm"], evs["xtm"]] + [b[1] for b in bs],
                      sig="s_mm")

        def hupd(e, bs=bs):
            inst = None
            if mode == "zero":
                e.tensor_copy(HST[:, 0:512], bs[0][0][:, :])
                inst = e.tensor_copy(HST[:, 512:1024], bs[1][0][:, :])
            else:
                for pr in range(16):
                    bb = bs[pr // 8][0]
                    col = (pr % 8) * 64
                    inst = e.scalar_tensor_tensor(
                        HST[:, pr * 64:(pr + 1) * 64], HST[:, pr * 64:(pr + 1) * 64],
                        ALC[:, pr:pr + 1], bb[:, col:col + 64], ALU.mult, ALU.add)
            return inst
        ev_h = R.op("vector", hupd, waits=[ev_smm, evs["alc"]], sig="s_ev")
        ring.rel(bs[0][2], ev_h); ring.rel(bs[1][2], ev_h)
        return ev_h

    gate = ev_zs_last
    for c in range(NC_):
        evs = chunk_prep(c, gate)
        gate = smm_update(evs, "zero" if c == 0 else "carry")
    ev_acore = R.op("vector", lambda e: e.tensor_reduce(
        ACOL[0:32, :], AAR[0:32, :], AX.X, ALU.add), waits=[ev_a_ready, gate],
        sig="s_ev")
    tap("score", HST[:, :], [gate])

    bt_, wt0_, tt0_ = ring.acq()
    ev_atp = R.op("tensor", lambda e, bt_=bt_: e.transpose(
        bt_[0:1, 0:32], ACOL[0:32, 0:1], IDF[0:32, 0:32]),
        waits=[ev_acore, wt0_], sig="s_mm")
    ev_arow = R.op("vector", lambda e, bt_=bt_: e.tensor_copy(
        WJS[0:1, 0:32], bt_[0:1, 0:32]), waits=[ev_atp], sig="s_ev")
    ring.rel(tt0_, ev_arow)
    ev_sdma = R.op("sync", lambda e: e.dma_start(d_sin[0:128, :], HST[:, :]),
                   waits=[gate], sig=usem(), inc=16)
    ev_adma = R.op("sync", lambda e: e.dma_start(d_sin[128:129, 0:32],
                                                 WJS[0:1, 0:32]),
                   waits=[ev_arow], sig=usem(), inc=16)
    ev_cc = R.op("gpsimd", lambda e: e.collective_compute(
        "AllGather", ALU.bypass, replica_groups=[list(range(NCORES))],
        ins=[d_sin.ap().opt()], outs=[d_sout.ap().opt()]),
        waits=[ev_sdma, ev_adma], sig="s_cc")
    ev_ag = R.op("sync", lambda e: e.dma_start(
        AGS[:, :], d_sout.ap().rearrange("(j r) c -> j r c", j=8)[:, 128, 0:32]),
        waits=[ev_cc], sig=usem(), inc=16)
    bw_, ww_, tw_ = ring.acq()
    ev_lw = R.op("tensor", lambda e, bw_=bw_: e.matmul(
        bw_[0:8, 0:32], CHAIN[:, :], AGS[0:8, :], start=True, stop=True),
        waits=[ev_ag, ww_, ev_const], sig="s_mm")
    ev_wj = R.op("scalar", lambda e, bw_=bw_: e.activation(
        WJS[0:8, :], bw_[0:8, 0:32], AF.Exp), waits=[ev_lw], sig="s_sc")
    ring.rel(tw_, ev_wj)
    ev_wjm = R.op("vector", lambda e: e.tensor_scalar(
        WJS[0:8, :], WJS[0:8, :], MKC[0:8, :], None, ALU.mult),
        waits=[ev_wj], sig="s_ev")
    bw2_, ww2_, tw2_ = ring.acq()
    ev_wt = R.op("tensor", lambda e, bw2_=bw2_: e.transpose(
        bw2_[0:32, 0:8], WJS[0:8, :], IDF[0:8, 0:8]), waits=[ev_wjm, ww2_], sig="s_mm")
    ev_wts = R.op("vector", lambda e, bw2_=bw2_: e.tensor_copy(
        WTS[0:32, :], bw2_[0:32, 0:8]), waits=[ev_wt], sig="s_ev")
    ring.rel(tw2_, ev_wts)
    bw3_, ww3_, tw3_ = ring.acq()
    def wcmm(e, bw3_=bw3_):
        inst = None
        for j in range(8):
            for pr in range(16):
                inst = e.matmul(bw3_[:, j * 16 + pr: j * 16 + pr + 1],
                                SELP[:, pr * 128:(pr + 1) * 128], WTS[0:32, j:j + 1],
                                start=True, stop=True)
        return inst
    ev_wc = R.op("tensor", wcmm, waits=[ev_wts, ww3_], sig="s_mm")
    ev_wcs = R.op("vector", lambda e, bw3_=bw3_: e.tensor_copy(
        WCOLS[:, :], bw3_[:, 0:128]), waits=[ev_wc], sig="s_ev")
    ring.rel(tw3_, ev_wcs)
    ev_hin = R.op("vector", lambda e: e.memset(HST[:, :], 0.0),
                  waits=[ev_sdma], sig="s_ev")
    ev_sjb_free = None
    for j in range(8):
        ev_sj = R.op("sync", lambda e, j=j: e.dma_start(
            SJB[:, :], d_sout[j * 129:j * 129 + 128, :]),
            waits=[ev_cc, ev_sjb_free], sig=usem(), inc=16)

        def hacc(e, j=j):
            inst = None
            for pr in range(16):
                inst = e.scalar_tensor_tensor(
                    HST[:, pr * 64:(pr + 1) * 64], SJB[:, pr * 64:(pr + 1) * 64],
                    WCOLS[:, j * 16 + pr: j * 16 + pr + 1],
                    HST[:, pr * 64:(pr + 1) * 64], ALU.mult, ALU.add)
            return inst
        ev_hin = R.op("vector", hacc, waits=[ev_sj, ev_wcs, ev_hin], sig="s_ev")
        ev_sjb_free = ev_hin
    ev_hbf = R.op("vector", lambda e: e.tensor_copy(HBF[:, :], HST[:, :]),
                  waits=[ev_hin], sig="s_ev")
    tap("hin", HST[:, :], [ev_hin])
    if phases <= 5:
        return _finish()

    ev_ygate_last = None
    ev_m2_prev = None
    ev_m3_prev = None
    mbuf_free = {0: None, 1: None}
    gate = ev_hbf
    snap("hbf2", HBF[:, :], [ev_hbf])
    for c in range(NC_):
        cw0 = c * 128
        evs = chunk_prep(c, gate)
        sev = []
        if c == 0:
            sev.append(snap("qch0", QCH[0:32, :], [evs["qch"]]))
            sev.append(snap("qtc0", QTC[:, :], [evs["qte"]]))
            sev.append(snap("dbs0", DBS[0:32, :], [evs["dbs"]]))
            sev.append(snap("ecs0", ECS[0:32, :], [evs["dbs"]]))
            sev.append(snap("alc0", ALC[:, :], [evs["alc"]]))
            sev.append(snap("xtm0", XTM[:, :], [evs["xtm"]]))
            sev.append(snap("btm0", BTM[:, :], [evs["btm"]]))
        if p2_lvl < 1:
            gate = evs["btm"]
            continue
        ev_ct = []
        for g4 in range(4):
            bc_, wc_, tc_ = ring.acq()
            def ecmm(e, g4=g4, bc_=bc_):
                inst = None
                for pp in range(4):
                    pr = g4 * 4 + pp
                    inst = e.matmul(bc_[:, pp * 128:(pp + 1) * 128],
                                    SELP[:, pr * 128:(pr + 1) * 128], ECS[0:32, :],
                                    start=True, stop=True)
                return inst
            ev_ecm = R.op("tensor", ecmm, waits=[evs["dbs"], wc_], sig="s_mm")

            def ctmul(e, g4=g4, bc_=bc_, cw0=cw0):
                src = CFm.ap().rearrange("p (pr t) -> p pr t", pr=16)[
                    :, g4 * 4:(g4 + 1) * 4, cw0:cw0 + 128]
                dst = CTLD.ap().rearrange("p (pr t) -> p pr t", pr=16)[
                    :, g4 * 4:(g4 + 1) * 4, 0:128]
                return e.tensor_tensor(dst, src,
                                       bc_.rearrange("p (pr t) -> p pr t", pr=4),
                                       ALU.mult)
            ev_c = R.op("vector", ctmul, waits=[ev_ecm, ev_cf_last, gate], sig="s_ev")
            ev_ct.append(ev_c)
            ring.rel(tc_, ev_c)
        if c == 0:
            sev.append(snap("ctld0", CTLD[:, :], [ev_ct[-1]]))
        if p2_lvl < 2:
            gate = evs["btm"]
            ev_ygate_last = ev_ct[-1]
            continue
        for g4 in range(8):
            h0 = g4 * 4
            bg0_, wg0_, tg0_ = ring.acq()
            bg1_, wg1_, tg1_ = ring.acq()
            bqr_, wqr_, tqr_ = ring.acq()
            bgs = (bg0_, bg1_)
            if p2_lvl < 20 and p2_lvl >= 10:
                def gmmX(e, h0=h0, bgs=bgs, cw0=cw0):
                    inst = None
                    for hh4 in range(4):
                        h = h0 + hh4
                        pr, hh = h // 2, h % 2
                        inst = e.matmul(
                            bgs[hh][:, (hh4 // 2) * 128:(hh4 // 2) * 128 + 128],
                            BFm[hh * 64:(hh + 1) * 64, pr * T + cw0: pr * T + cw0 + 128],
                            CFm[hh * 64:(hh + 1) * 64, pr * T + cw0: pr * T + cw0 + 128],
                            start=True, stop=True)
                    return inst
                ev_gm = R.op("tensor", gmmX, waits=[ev_bf_last, ev_cf_last, wg0_, wg1_],
                             sig="s_mm")
                evr = R.op("vector", lambda e, bgs=bgs: (
                    e.tensor_copy(ES2[:, 0:256], bgs[0][:, 0:256]),
                    e.tensor_copy(ES2[:, 256:512], bgs[1][:, 0:256]))[-1],
                    waits=[ev_gm], sig="s_ev")
                ring.rel(tg0_, evr); ring.rel(tg1_, evr)
                if p2_lvl >= 14:
                    def qrmmX(e, h0=h0, bqr_=bqr_):
                        inst = None
                        for hh4 in range(4):
                            sel = IDF[0:32, h0 + hh4:h0 + hh4 + 1].broadcast_to([32, 128])
                            inst = e.matmul(bqr_[:, hh4 * 128:(hh4 + 1) * 128], sel,
                                            QCH[0:32, :], start=True, stop=True)
                        return inst
                    ev_qr = R.op("tensor", qrmmX, waits=[evs["qch"], wqr_], sig="s_mm")
                    if p2_lvl >= 16:
                        def m1X(e, h0=h0, bqr_=bqr_):
                            inst = None
                            for hh4 in range(4):
                                inst = e.tensor_scalar(
                                    DS2[:, hh4 * 128:(hh4 + 1) * 128],
                                    bqr_[:, hh4 * 128:(hh4 + 1) * 128],
                                    QTC[:, h0 + hh4:h0 + hh4 + 1], None, ALU.subtract)
                            return inst
                        evq = R.op("vector", m1X, waits=[ev_qr, evs["qte"]], sig="s_ev")
                    else:
                        evq = R.op("vector", lambda e, bqr_=bqr_: e.tensor_copy(
                            DS2[:, :], bqr_[:, :]), waits=[ev_qr], sig="s_ev")
                    ring.rel(tqr_, evq)
                    ev_ygate_last = evq
                else:
                    ring.rel(tqr_, wqr_)
                    ev_ygate_last = evr
                continue

            def gmm(e, h0=h0, bgs=bgs, cw0=cw0):
                inst = None
                for hh4 in range(4):
                    h = h0 + hh4
                    pr, hh = h // 2, h % 2
                    tgt = bgs[hh][:, (hh4 // 2) * 128:(hh4 // 2) * 128 + 128]
                    e.matmul(tgt[0:1, 0:1], IDB[:, 0:1], IDB[:, 0:1],
                             start=True, stop=True)
                    inst = e.matmul(
                        tgt,
                        BFm[hh * 64:(hh + 1) * 64, pr * T + cw0: pr * T + cw0 + 128],
                        CFm[hh * 64:(hh + 1) * 64, pr * T + cw0: pr * T + cw0 + 128],
                        start=True, stop=True)
                return inst
            ev_gm = R.op("tensor", gmm, waits=[ev_bf_last, ev_cf_last, wg0_, wg1_],
                         sig="s_mm")
            if c == 0 and g4 == 0 and "bg00" in dbg_taps:
                ev_bgc = R.op("vector", lambda e, bg0_=bg0_: e.tensor_copy(
                    MTR[1][:, :].bitcast(F32)[:, 0:256], bg0_[:, 0:256]),
                    waits=[ev_gm, mbuf_free[1]], sig="s_ev")
                sv = snap("bg00", MTR[1][:, :].bitcast(F32)[:, 0:256], [ev_bgc])
                mbuf_free[1] = [sv]

            def qrmm(e, h0=h0, bqr_=bqr_):
                e.matmul(bqr_[0:1, 0:1], IDB[:, 0:1], IDB[:, 0:1],
                         start=True, stop=True)
                inst = None
                for hh4 in range(4):
                    sel = IDF[0:32, h0 + hh4:h0 + hh4 + 1].broadcast_to([32, 128])
                    inst = e.matmul(bqr_[:, hh4 * 128:(hh4 + 1) * 128], sel,
                                    QCH[0:32, :], start=True, stop=True)
                return inst
            ev_qr = R.op("tensor", qrmm, waits=[evs["qch"], wqr_], sig="s_mm")

            def m1(e, h0=h0, bqr_=bqr_):
                inst = None
                for hh4 in range(4):
                    inst = e.tensor_scalar(
                        DS2[:, hh4 * 128:(hh4 + 1) * 128],
                        bqr_[:, hh4 * 128:(hh4 + 1) * 128],
                        QTC[:, h0 + hh4:h0 + hh4 + 1], None, ALU.subtract)
                return inst
            ev_m1 = R.op("vector", m1, waits=[ev_qr, evs["qte"], ev_m2_prev],
                         sig="s_ev")
            ring.rel(tqr_, ev_m1)

            def m2(e):
                e.activation(ES2[:, :], DS2[:, :], AF.Relu, scale=-1.0)
                return e.activation(ES2[:, :], ES2[:, :], AF.Exp, scale=-1.0)
            ev_m2 = R.op("scalar", m2, waits=[ev_m1, ev_m3_prev], sig="s_sc")
            ev_m2_prev = ev_m2

            mbuf = MTR[g4 % 2]
            def m3(e, bgs=bgs, mbuf=mbuf):
                inst = None
                for hh4 in range(4):
                    sl_ = slice(hh4 * 128, (hh4 + 1) * 128)
                    e.tensor_tensor(ES2[:, sl_], ES2[:, sl_], CMASK[:, :], ALU.mult)
                    inst = e.tensor_tensor(
                        mbuf[:, sl_],
                        bgs[hh4 % 2][:, (hh4 // 2) * 128:(hh4 // 2) * 128 + 128],
                        ES2[:, sl_], ALU.mult)
                return inst
            ev_m3 = R.op("vector", m3, waits=[ev_m2, ev_gm, mbuf_free[g4 % 2]],
                         sig="s_ev")
            ev_m3_prev = ev_m3
            ring.rel(tg0_, ev_m3); ring.rel(tg1_, ev_m3)
            if c == 0 and g4 == 0:
                sv1 = snap("mt00", mbuf[:, :], [ev_m3])
                sv2 = snap("es00", ES2[:, :], [ev_m3])
                ev_m3_prev = [ev_m3, sv2]
                ev_m3 = [ev_m3, sv1]
            if p2_lvl < 3:
                mbuf_free[g4 % 2] = ev_m3
                ev_ygate_last = ev_m3
                continue

            by_, wy_, ty_ = ring.acq()
            def ymm(e, h0=h0, by_=by_, mbuf=mbuf):
                inst = None
                for hh4 in range(4):
                    h = h0 + hh4
                    pr, hh = h // 2, h % 2
                    pp = pr - h0 // 2
                    tgt = by_[hh * 64:(hh + 1) * 64, pp * 128:(pp + 1) * 128]
                    e.matmul(tgt, XTM[:, h * 64:(h + 1) * 64],
                             mbuf[:, hh4 * 128:(hh4 + 1) * 128],
                             start=True, stop=False,
                             tile_position=(0, 64) if hh else None)
                    inst = e.matmul(tgt,
                                    HBF[hh * 64:(hh + 1) * 64, pr * 64:(pr + 1) * 64],
                                    CTLD[hh * 64:(hh + 1) * 64, pr * 128:(pr + 1) * 128],
                                    start=False, stop=True,
                                    tile_position=(64, 64) if hh else None)
                return inst
            ev_ym = R.op("tensor", ymm,
                         waits=[ev_m3, evs["xtm"], ev_ct[g4 // 2], gate, wy_],
                         sig="s_mm")
            mbuf_free[g4 % 2] = ev_ym
            if p2_lvl < 4:
                ev_ev = R.op("vector", lambda e, by_=by_: e.tensor_copy(
                    ES2[:, 0:256], by_[:, 0:256]), waits=[ev_ym],
                    sig="s_ev")
                ring.rel(ty_, ev_ev)
                ev_ygate_last = ev_ev
                continue

            def gv(e, h0=h0, by_=by_, cw0=cw0):
                inst = None
                for pp in range(2):
                    pr = h0 // 2 + pp
                    xsl = XF[:, pr * T + cw0: pr * T + cw0 + 128]
                    psl = by_[:, pp * 128:(pp + 1) * 128]
                    e.scalar_tensor_tensor(xsl, xsl, COLS[:, 32 + pr:33 + pr], psl,
                                           ALU.mult, ALU.add)
                    inst = e.tensor_tensor(
                        xsl, xsl, ZS[:, pr * T + cw0: pr * T + cw0 + 128], ALU.mult)
                return inst
            ev_ge = R.op("vector", gv, waits=[ev_ym, ev_zs_last], sig="s_ev")
            ring.rel(ty_, ev_ge)
            ev_ygate_last = ev_ge
        if p2_lvl < 5:
            gate = [evs["btm"]] + sev
            continue
        ev_h2 = smm_update(evs, "carry")
        gate = R.op("vector", lambda e: e.tensor_copy(HBF[:, :], HST[:, :]),
                    waits=[ev_h2], sig="s_ev")
        gate = [gate] + sev
    tap("ygate", XF[:, :], [ev_ygate_last])
    if phases <= 6:
        return _finish()

    ev_uall = []
    for tt in range(8):
        ev_uall.append(R.op("sync", lambda e, tt=tt: e.dma_start(
            UALL[:, tt * D:(tt + 1) * D], d_u[3 + tt * 128: 3 + (tt + 1) * 128, :]),
            waits=[ev_ygate_last, gate], sig=usem(), inc=16))
    ev_wo = [None] * 3
    ev_wo_free = [ev_ygate_last] * 3
    ev_gb = [None, None]
    ev_gb_free = [ev_ygate_last, ev_ygate_last]
    opi_free = {i: None for i in range(4)}
    n_out = 0
    for ot in range(16):
        sl = ot % 3
        ev_wo[sl] = R.op("gpsimd", ld(
            WOR[sl][:, :], d_wouts[ot].rearrange("p a b -> p (a b)")),
            waits=[ev_wo_free[sl]], sig=wsem(), inc=16)
        gsl = ot % 2
        ev_gb[gsl] = R.op("sync", lambda e, ot=ot, gsl=gsl: e.dma_start(
            GBUF[gsl][:, :], d_g[ot * 128:(ot + 1) * 128, :]),
            waits=[ev_g_store[ot], ev_gb_free[gsl]], sig=usem(), inc=16)
        bo = [ring.acq(), ring.acq()]

        def womm(e, sl=sl, bo=bo):
            inst = None
            for (bi, off) in ((0, 0), (1, 512)):
                for kt in range(KT):
                    inst = e.matmul(bo[bi][0][:, :], WOR[sl][:, kt * 128:(kt + 1) * 128],
                                    XF[:, kt * T + off: kt * T + off + 512],
                                    start=(kt == 0), stop=(kt == KT - 1))
            return inst
        ev_wmm = R.op("tensor", womm,
                      waits=[ev_wo[sl], ev_ygate_last] + [b[1] for b in bo], sig="s_mm")
        ev_wo_free[sl] = ev_wmm

        def sgmul(e, bo=bo, gsl=gsl):
            e.tensor_tensor(SGT[:, 0:512], bo[0][0][:, :], GBUF[gsl][:, 0:512], ALU.mult)
            return e.tensor_tensor(SGT[:, 512:1024], bo[1][0][:, :],
                                   GBUF[gsl][:, 512:1024], ALU.mult)
        ev_sgm = R.op("vector", sgmul, waits=[ev_wmm, ev_gb[gsl]], sig="s_ev")
        ring.rel(bo[0][2], ev_sgm); ring.rel(bo[1][2], ev_sgm)
        ev_gb_free[gsl] = ev_sgm

        for tt in range(8):
            btx_, wtx_, ttx_ = ring.acq()
            ev_tx = R.op("tensor", lambda e, tt=tt, btx_=btx_: e.transpose(
                btx_[:, 0:128], SGT[:, tt * 128:(tt + 1) * 128], IDF[:, :]),
                waits=[ev_sgm, wtx_], sig="s_mm")
            oi_ = (ot * 8 + tt) % 4
            opi = OPI[oi_]
            ev_add = R.op("vector", lambda e, tt=tt, ot=ot, btx_=btx_, opi=opi:
                          e.tensor_tensor(opi[:, :], btx_[:, 0:128],
                                          UALL[:, tt * D + ot * 128: tt * D + (ot + 1) * 128],
                                          ALU.add),
                          waits=[ev_tx, ev_uall[tt], opi_free[oi_]], sig="s_ev")
            ring.rel(ttx_, ev_add)
            q_eng = "sync" if (ot * 8 + tt) % 2 == 0 else "scalar"
            ev_o = R.op(q_eng, lambda e, tt=tt, ot=ot, opi=opi: e.dma_start(
                d_out[tt * 128:(tt + 1) * 128, ot * 128:(ot + 1) * 128], opi[:, :]),
                waits=[ev_add], sig="s_out", inc=16)
            opi_free[oi_] = ev_o
            n_out += 1

    return _finish(n_out)


_SEMS = ["s_w0", "s_w1", "s_w2", "s_u0", "s_u1", "s_u2", "s_mm", "s_ev", "s_sc", "s_cc", "s_out", "s_gp"]


def prep_inputs(inputs):
    f32 = np.float32
    u = np.asarray(inputs["L0_attn_u"], f32)
    W_in = np.asarray(inputs["L0_attn_W_in"], f32)
    conv_w = np.asarray(inputs["L0_attn_conv_w"], f32)
    conv_b = np.asarray(inputs["L0_attn_conv_b"], f32)
    z_bias = np.asarray(inputs["L0_attn_z_bias"], f32)
    Dv = np.asarray(inputs["L0_attn_D"], f32)
    W_out = np.asarray(inputs["L0_attn_W_out"], f32)
    gate_w = np.asarray(inputs["L0_ffn_gate_w"], f32)
    sw1 = np.asarray(inputs["L0_ffn_sw1"], f32)
    sw2 = np.asarray(inputs["L0_ffn_sw2"], f32)
    sw3 = np.asarray(inputs["L0_ffn_sw3"], f32)
    rw1 = np.asarray(inputs["L0_ffn_rw1"], f32)
    rw2 = np.asarray(inputs["L0_ffn_rw2"], f32)
    rw3 = np.asarray(inputs["L0_ffn_rw3"], f32)

    C = np.ascontiguousarray

    W_pad = np.zeros((8320, D), f32)
    W_pad[:8224] = W_in
    wins = C(W_pad.reshape(65, 128, KT, 128).transpose(0, 3, 2, 1))
    wouts = C(W_out.reshape(16, 128, KT, 128).transpose(0, 3, 2, 1))

    w13 = np.empty((NE, FT, 128, 32, 128), f32)
    w2q = np.empty((NE, 4, 128, FT, 512), f32)
    for e in range(NE):
        if e < 2:
            w1f = sw1[:, e * 1408:(e + 1) * 1408]
            w3f = sw3[:, e * 1408:(e + 1) * 1408]
            w2f = sw2[e * 1408:(e + 1) * 1408, :]
        else:
            w1f, w3f, w2f = rw1[e - 2], rw3[e - 2], rw2[e - 2]
        w1b = w1f.reshape(KT, 128, FT, 128).transpose(2, 1, 0, 3)
        w3b = w3f.reshape(KT, 128, FT, 128).transpose(2, 1, 0, 3)
        w13[e, :, :, 0:16, :] = w1b
        w13[e, :, :, 16:32, :] = w3b
        w2q[e] = w2f.reshape(FT, 128, 4, 512).transpose(2, 1, 0, 3)
    w13 = C(w13)
    w2q = C(w2q)

    diag = np.zeros((48, 128, 4, 128), f32)
    cw3 = conv_w[:, 0, :].reshape(48, 128, 4)
    ii = np.arange(128)
    diag[:, ii, :, ii] = cw3.transpose(1, 0, 2)

    cols = np.zeros((128, 64), f32)
    cols[:, 0] = 1e-5
    cols[:, 1] = 1.0
    cols[:, 16:32] = z_bias.reshape(16, 128).T
    pr_idx = np.arange(16)
    pp_ = np.arange(128)
    cols[:, 32:48] = Dv[2 * pr_idx[None, :] + (pp_[:, None] // 64)]
    convb = C(conv_b.reshape(48, 128).T)
    gatew = C(gate_w.reshape(KT, 128, 8).transpose(1, 0, 2).reshape(128, 128))

    selp = np.zeros((32, 2048), f32)
    for pr in range(16):
        selp[2 * pr, pr * 128:pr * 128 + 64] = 1.0
        selp[2 * pr + 1, pr * 128 + 64:pr * 128 + 128] = 1.0
    cmask = np.triu(np.ones((128, 128), f32))
    idf = np.eye(128, dtype=f32)
    idb = np.eye(128, dtype=ml_dtypes.bfloat16)

    shared = {"wins": wins, "wouts": wouts, "w13": w13, "w2q": w2q, "diag": diag,
              "idf": idf, "idb": idb, "cmask": cmask, "cols": cols, "convb": convb,
              "gatew": gatew, "selp": selp}

    in_maps = []
    for core in range(NCORES):
        b, qq = core // 4, core % 4
        uext = np.zeros((TE, D), f32)
        t0 = qq * T
        if qq > 0:
            uext[0:3] = u[b, t0 - 3:t0]
        uext[3:] = u[b, t0:t0 + T]
        ch = np.zeros((8, 8), f32)
        mk = np.zeros((8, 1), f32)
        for j in range(4 * b, core):
            mk[j, 0] = 1.0
            for m in range(j + 1, core):
                ch[m, j] = 1.0
        m = dict(shared)
        m["uext"] = C(uext)
        m["chain"] = ch
        m["mkcol"] = mk
        in_maps.append(m)
    return in_maps


_CACHE = {}


def _run(inputs, dbg_taps=(), phases=99, p2_lvl=99):
    key = (tuple(sorted(dbg_taps)), phases, p2_lvl)
    if key not in _CACHE:
        nc, R = build_program(dbg_taps, phases, p2_lvl)
        sems = {}
        from contextlib import ExitStack
        with ExitStack() as st:
            for s in _SEMS:
                sems[s] = st.enter_context(nc.semaphore(s))
            R.emit(nc, sems)
        _CACHE[key] = nc
    nc = _CACHE[key]
    in_maps = prep_inputs(inputs)
    res = run_bass_kernel_spmd(nc, in_maps, core_ids=list(range(NCORES)))
    return res


def kernel(**inputs):
    res = _run(inputs)
    out = np.empty((2, 4096, D), np.float32)
    for core in range(NCORES):
        b, qq = core // 4, core % 4
        out[b, qq * T:(qq + 1) * T] = res.results[core]["out"]
    return out
